# version 29
# baseline (speedup 1.0000x reference)
import numpy as np
import ml_dtypes
from contextlib import ExitStack

import concourse.bass as bass
import concourse.mybir as mybir
from concourse.bass_utils import run_bass_kernel_spmd

ENGS = {"v": "vector", "s": "scalar", "t": "tensor", "g": "gpsimd", "y": "sync"}


class _EngProxy:
    def __init__(self, pk, eng_key):
        self._pk = pk
        self._key = eng_key
        self._eng = getattr(pk.nc, ENGS[eng_key])

    def __getattr__(self, name):
        fn = getattr(self._eng, name)
        if not callable(fn):
            return fn
        pk, key = self._pk, self._key

        def wrap(*args, nosync=False, **kwargs):
            st = pk.state[key]
            waits = list(st["pend"])
            st["pend"] = []
            if not nosync and st["n"] > 0:
                waits.append((pk.self_sems[key], st["n"]))
            dedup = {}
            for sem, val in waits:
                kk = id(sem)
                if kk not in dedup or dedup[kk][1] < val:
                    dedup[kk] = (sem, val)
            waits = list(dedup.values())
            for sem, val in waits[1:]:
                self._eng.wait_ge(sem, val)
            ins = fn(*args, **kwargs)
            if not isinstance(ins, bass.BassInstruction):
                return ins
            if waits:
                ins._wait_ge(waits[0][0], waits[0][1])
            ins.then_inc(pk.self_sems[key], 1)
            st["n"] += 1
            return ins

        return wrap


class PK:
    def __init__(self, nc, ctx):
        self.nc = nc
        self.ctx = ctx
        self.self_sems = {}
        self.state = {}
        for k in ("v", "s", "t", "g", "y"):
            self.state[k] = {"n": 0, "pend": []}
        for k in ("v", "s", "t", "g"):
            self.self_sems[k] = ctx.enter_context(nc.semaphore(f"self_{k}"))
        self.dma_sem_handles = {}
        self.dma_counts = {}
        self.v = _EngProxy(self, "v")
        self.s = _EngProxy(self, "s")
        self.t = _EngProxy(self, "t")
        self.g = _EngProxy(self, "g")

    def mark(self, eng_key):
        if eng_key in self.self_sems:
            return (self.self_sems[eng_key], self.state[eng_key]["n"])
        raise KeyError(eng_key)

    def dmark(self, queue_name):
        return (self.dma_sem_handles[queue_name], self.dma_counts[queue_name])

    def pend(self, eng_key, token):
        self.state[eng_key]["pend"].append(token)

    def wait_now(self, eng_key, token):
        eng = getattr(self.nc, ENGS[eng_key])
        eng.wait_ge(token[0], token[1])

    def switch(self, eng_keys, n, index_fn=None, pad=100000):
        nc = self.nc
        engs = [getattr(nc, ENGS[k]) for k in eng_keys]
        regs = []
        for k, eng in zip(eng_keys, engs):
            pid = nc.partition_id(engines=[eng.engine])
            expr = index_fn(pid) if index_fn else (pid % n)
            regs.append(eng.compute_val(expr))
        snap0 = {k: self.state[k]["n"] for k in eng_keys}
        dsnap0 = dict(self.dma_counts)
        finals = []
        dfinals = []
        for i in nc.Switch(engines=engs, index=regs, n=n):
            for k in eng_keys:
                self.state[k]["n"] = snap0[k]
                self.state[k]["pend"] = []
            self.dma_counts.update(dsnap0)
            yield i
            finals.append({k: self.state[k]["n"] for k in eng_keys})
            dfinals.append(dict(self.dma_counts))
        for f in finals[1:]:
            assert f == finals[0], (
                f"switch bodies must issue equal per-engine instruction "
                f"counts; got {finals}")
        for df in dfinals[1:]:
            assert df == dfinals[0], (
                f"switch bodies must issue equal DMA counts; got {dfinals}")
        for k in eng_keys:
            self.state[k]["n"] = finals[0][k]
            self.state[k]["pend"] = []
        self.dma_counts.update(dfinals[0])

    def dma(self, queue_key, out, in_, queue_name=None, flush=True, **kw):
        if queue_name is None:
            queue_name = f"dma_{queue_key}"
        if queue_name not in self.dma_sem_handles:
            self.dma_sem_handles[queue_name] = self.ctx.enter_context(
                self.nc.semaphore(queue_name))
            self.dma_counts[queue_name] = 0
        sem = self.dma_sem_handles[queue_name]
        eng = getattr(self.nc, ENGS[queue_key])
        st = self.state[queue_key]
        pends = st["pend"]
        st["pend"] = []
        for s2, val in pends[1:]:
            eng.wait_ge(s2, val)
        ins = eng.dma_start(out=out, in_=in_, **kw)
        if pends:
            ins._wait_ge(pends[0][0], pends[0][1])
        ins.then_inc(sem, 16)
        self.dma_counts[queue_name] += 16
        tok = (sem, self.dma_counts[queue_name])
        if flush:
            eng.wait_ge(sem, tok[1])
        return tok

    def dma_flush_all(self, queue_key):
        eng = getattr(self.nc, ENGS[queue_key])
        for queue_name, sem in self.dma_sem_handles.items():
            eng.wait_ge(sem, self.dma_counts[queue_name])

    def barrier(self):
        self.dma_flush_all("y")
        self.dma_flush_all("g")
        toks = [(self.self_sems[k], self.state[k]["n"]) for k in ("t", "s", "v")]
        toks += [(self.dma_sem_handles[q], self.dma_counts[q])
                 for q in self.dma_sem_handles]
        for k in ("t", "s", "v", "g", "y"):
            eng = getattr(self.nc, ENGS[k])
            own = self.self_sems.get(k)
            for sem, val in toks:
                if sem is own or val == 0:
                    continue
                eng.wait_ge(sem, val)


F32 = mybir.dt.float32
BF16 = mybir.dt.bfloat16
AF = mybir.ActivationFunctionType
ALU = mybir.AluOpType
AX = mybir.AxisListType.X

S = 4096
D = 1024
NH = 16
I = 2816
M0 = 8.0
SM_SCALE = 0.125
EPS = 1e-5
POS = {0: [0, 1, 6, 7], 1: [2, 3, 4, 5]}
THRESH = [0.25, 0.75, 1.25, 1.75, 2.5, 3.5, 5.0]
CONSTS = (("one", 1.0), ("zero", 0.0), ("mhalf", -0.5), ("m1", -1.0),
          ("smsc", SM_SCALE), ("mM0", -M0))


def build_program(phases=4):
    nc = bass.Bass("TRN2", target_bir_lowering=False)
    ap = lambda t: t.ap()
    x_seq = ap(nc.dram_tensor("x_seq", [S, D], F32, kind="ExternalInput"))
    x_own = ap(nc.dram_tensor("x_own", [2048, D], F32, kind="ExternalInput"))
    wq = ap(nc.dram_tensor("wq", [D, D], F32, kind="ExternalInput"))
    wk = ap(nc.dram_tensor("wk", [D, D], F32, kind="ExternalInput"))
    wv = ap(nc.dram_tensor("wv", [D, D], F32, kind="ExternalInput"))
    wo = ap(nc.dram_tensor("wo", [D, D], F32, kind="ExternalInput"))
    bo = ap(nc.dram_tensor("bo", [D], F32, kind="ExternalInput"))
    wg = ap(nc.dram_tensor("wg", [I, D], F32, kind="ExternalInput"))
    wu = ap(nc.dram_tensor("wu", [I, D], F32, kind="ExternalInput"))
    wd = ap(nc.dram_tensor("wd", [D, I], F32, kind="ExternalInput"))
    masks = ap(nc.dram_tensor("masks", [128, 4 * 512], BF16, kind="ExternalInput"))
    ident_in = ap(nc.dram_tensor("ident_in", [128, 128], BF16, kind="ExternalInput"))
    out = ap(nc.dram_tensor("out", [2048, D], F32, kind="ExternalOutput"))
    q_scr = ap(nc.dram_tensor("q_scr", [4, 128, 8 * 512], BF16, kind="Internal"))
    xq_scr = ap(nc.dram_tensor("xq_scr", [4, 128, 4 * D], BF16, kind="Internal"))
    xn_scr = ap(nc.dram_tensor("xn_scr", [4, 128, 4 * D], F32, kind="Internal"))
    ay_scr = ap(nc.dram_tensor("ay_scr", [4, 128, 8 * 512], BF16, kind="Internal"))
    wo_scr = ap(nc.dram_tensor("wo_scr", [8, 128, D], BF16, kind="Internal"))
    hqT_scr = ap(nc.dram_tensor("hqT_scr", [16, 22, 128, 128], BF16, kind="Internal"))
    wgq_scr = ap(nc.dram_tensor("wgq_scr", [22, 128, D], BF16, kind="Internal"))
    wuq_scr = ap(nc.dram_tensor("wuq_scr", [22, 128, D], BF16, kind="Internal"))
    wdq_scr = ap(nc.dram_tensor("wdq_scr", [8, 128, I], BF16, kind="Internal"))

    with ExitStack() as ctx:
        pk = PK(nc, ctx)
        sbuf = lambda nm, shape, dt=F32: ctx.enter_context(nc.sbuf_tensor(nm, shape, dt))

        bo_t = sbuf("bo_t", [128, 8], F32)
        ones1 = sbuf("ones1", [1, 64], BF16)
        ident = sbuf("ident", [128, 128], BF16)
        junk = sbuf("junk", [128, 256], F32)
        cst = {nm: sbuf(f"c_{nm}", [128, 1], F32) for nm, _ in CONSTS}

        t_ident = pk.dma("y", out=ident[:, :], in_=ident_in[:, :])
        t_bo = None
        for dc in range(8):
            t_bo = pk.dma("y", out=bo_t[:, dc:dc + 1], in_=bo[dc * 128:(dc + 1) * 128],
                          flush=(dc == 7))
        for nm, val in CONSTS:
            pk.v.memset(cst[nm][:, :], val, nosync=True)
        pk.v.memset(ones1[:, :], 1.0, nosync=True)
        t_const = pk.mark("v")
        pk.wait_now("t", t_ident)
        pk.wait_now("t", t_const)
        pk.wait_now("v", t_bo)
        pk.wait_now("s", t_const)

        lnhold = {}
        lns2 = {nm: sbuf(f"ln_{nm}", [128, 2], F32) for nm in
                ("sum", "sq", "m", "msq", "var", "vare", "lnv", "rstd", "nb1")}

        def emit_ln(x_tile, outs):
            x3 = x_tile.rearrange("p (s d) -> p s d", d=D)
            pk.v.tensor_reduce(lns2["sum"][:, :].rearrange("p (s o) -> p s o", o=1),
                               x3, axis=AX, op=ALU.add)
            pk.s.activation(lnhold["sink"][:, :], x_tile[:, 0:D], AF.Square,
                            accum_out=lns2["sq"][:, 0:1])
            pk.s.activation(lnhold["sink"][:, :], x_tile[:, D:2 * D], AF.Square,
                            accum_out=lns2["sq"][:, 1:2])
            pk.v.tensor_scalar(lns2["m"][:, :], lns2["sum"][:, :], 1.0 / D, None,
                               op0=ALU.mult)
            pk.v.tensor_tensor(lns2["msq"][:, :], lns2["m"][:, :], lns2["m"][:, :],
                               op=ALU.mult)
            pk.pend("v", pk.mark("s"))
            pk.v.scalar_tensor_tensor(lns2["var"][:, :], lns2["sq"][:, :], 1.0 / D,
                                      lns2["msq"][:, :], op0=ALU.mult,
                                      op1=ALU.subtract)
            pk.v.tensor_scalar(lns2["vare"][:, :], lns2["var"][:, :], EPS, None,
                               op0=ALU.add)
            pk.pend("s", pk.mark("v"))
            pk.s.activation(lns2["lnv"][:, :], lns2["vare"][:, :], AF.Ln)
            pk.s.activation(lns2["rstd"][:, :], lns2["lnv"][:, :], AF.Exp,
                            scale=cst["mhalf"][:, :])
            pk.pend("v", pk.mark("s"))
            pk.v.tensor_tensor(lns2["nb1"][:, :], lns2["m"][:, :], lns2["rstd"][:, :],
                               op=ALU.mult)
            pk.v.tensor_scalar(lns2["msq"][:, :], lns2["nb1"][:, :], -1.0, None,
                               op0=ALU.mult)
            for sub in range(2):
                for dst in outs[sub]:
                    pk.v.tensor_scalar(dst, x_tile[:, sub * D:(sub + 1) * D],
                                       lns2["rstd"][:, sub:sub + 1],
                                       lns2["msq"][:, sub:sub + 1],
                                       op0=ALU.mult, op1=ALU.add)

        def make_quant_bufs(alloc, width, tag):
            sm = {nm: alloc(f"q_{nm}_{tag}", [128, width // 16], F32)
                  for nm in ("am", "sc", "rc")}
            sm["scb"] = alloc(f"q_scb_{tag}", [128, width // 16], BF16)
            big = [alloc(f"qb_f{i}_{tag}", [128, width], F32) for i in range(2)]
            big += [alloc(f"qb_{i}_{tag}", [128, width], BF16) for i in range(11)]
            return (sm, big, width)

        NPOOL = 4
        QBIG = 2.0 ** 96
        blk16 = lambda t: t.rearrange("p (nb b) -> p nb b", b=16)

        def q_slices(bufs, width):
            (qsm, big, bw) = bufs
            assert width <= bw
            xn, x2 = (b[:, 0:width] for b in big[0:2])
            l1, l2, mx = (b[:, 0:width] for b in big[2:5])
            cbuf = [b[:, 0:width] for b in big[5:12]]
            sgb = big[12][:, 0:width]
            nb = width // 16
            am, sc, rc = (qsm[nm][:, 0:nb] for nm in ("am", "sc", "rc"))
            scb = qsm["scb"][:, 0:nb]
            return xn, x2, l1, l2, mx, cbuf, sgb, nb, am, sc, rc, scb

        def emit_quant_head(src, width, bufs, pool=False, npool=NPOOL):
            xn, x2, l1, l2, mx, cbuf, sgb, nb, am, sc, rc, scb = \
                q_slices(bufs, width)
            bc = lambda t: t.rearrange("p (nb o) -> p nb o", o=1) \
                .broadcast_to([128, nb, 16])
            pk.v.tensor_reduce(am.rearrange("p (nb o) -> p nb o", o=1),
                               src.rearrange("p (nb b) -> p nb b", b=16),
                               axis=AX, op=ALU.max, apply_absolute_value=True)
            pk.v.tensor_scalar(sc, am, 1.0 / 6.0, 1e-12, op0=ALU.mult, op1=ALU.max)
            pk.v.reciprocal(rc, sc)
            pk.v.tensor_copy(scb, sc)
            pk.pend("v", pk.mark("s"))
            pk.v.tensor_tensor(blk16(xn), src.rearrange("p (nb b) -> p nb b", b=16),
                               bc(rc), op=ALU.mult)
            pk.pend("s", pk.mark("v"))
            pk.s.activation(x2, xn, AF.Square)
            pk.s.activation(sgb, xn, AF.Sign, nosync=True)
            st = {"x2": pk.mark("s"), "gcmp": None, "npool": npool}
            if pool:
                gtmp = [xn, l1, l2, mx]
                t_gclamp = None
                for b0 in range(0, npool, 4):
                    bi = list(range(b0, min(b0 + 4, npool)))
                    for i in bi:
                        th = THRESH[i]
                        pk.pend("g", st["x2"])
                        if t_gclamp is not None:
                            pk.pend("g", t_gclamp)
                        pk.g.tensor_scalar(gtmp[i % 4], x2, QBIG,
                                           -th * th * QBIG,
                                           op0=ALU.mult, op1=ALU.add,
                                           nosync=(i % 4 > 0))
                    t_gsub = pk.mark("g")
                    for i in bi:
                        pk.pend("g", t_gsub)
                        pk.g.tensor_scalar(cbuf[i], gtmp[i % 4], 1.0, 0.0,
                                           op0=ALU.min, op1=ALU.max,
                                           nosync=(i % 4 > 0))
                    t_gclamp = pk.mark("g")
                st["gcmp"] = pk.mark("g")
            return st

        def emit_quant_tail(dst, width, bufs, st, pool=False):
            xn, x2, l1, l2, mx, cbuf, sgb, nb, am, sc, rc, scb = \
                q_slices(bufs, width)
            bc = lambda t: t.rearrange("p (nb o) -> p nb o", o=1) \
                .broadcast_to([128, nb, 16])
            lo = st["npool"] if pool else 0
            for i, th in enumerate(THRESH[lo:7]):
                pk.pend("v", st["x2"])
                pk.v.tensor_scalar(cbuf[lo + i], x2, th * th, None,
                                   op0=ALU.is_gt, nosync=(i > 0))
            if pool:
                pk.pend("v", st["gcmp"])
            pk.v.tensor_tensor(l1, cbuf[0], cbuf[1], op=ALU.add)
            pk.v.tensor_tensor(l2, cbuf[2], cbuf[3], op=ALU.add)
            pk.v.tensor_tensor(mx, cbuf[4], cbuf[5], op=ALU.add)
            pk.v.tensor_tensor(cbuf[0], l1, l2, op=ALU.add)
            qa, u = cbuf[3], cbuf[4]
            pk.v.scalar_tensor_tensor(cbuf[1], cbuf[0], 0.5, mx,
                                      op0=ALU.mult, op1=ALU.add)
            pk.v.scalar_tensor_tensor(qa, cbuf[6], 2.0, cbuf[1],
                                      op0=ALU.mult, op1=ALU.add)
            pk.v.tensor_tensor(blk16(u), blk16(qa), bc(scb), op=ALU.mult)
            pk.v.tensor_tensor(dst, u, sgb, op=ALU.mult)

        def emit_quant(src, dst, width, bufs):
            st = emit_quant_head(src, width, bufs)
            emit_quant_tail(dst, width, bufs, st)

        def dma_x_256(dst, src_rows, pend_tok=None):
            tok = None
            for sub in range(2):
                if pend_tok is not None:
                    pk.pend("y", pend_tok)
                    pend_tok = None
                tok = pk.dma("y", out=dst[:, sub * D:(sub + 1) * D],
                             in_=src_rows[sub * 128:(sub + 1) * 128, :],
                             flush=(sub == 1))
            return tok

        with ExitStack() as attn_scope:
            sba = lambda nm, shape, dt=F32: attn_scope.enter_context(
                nc.sbuf_tensor(nm, shape, dt))
            kf = sba("kf", [128, 8 * S], BF16)
            vt = sba("vt", [128, 32 * 1040], BF16)
            masks_t = sba("masks_t", [128, 128], BF16)
            t_mask = pk.dma("y", out=masks_t[:, :], in_=masks[:, 0:128])
            pk.pend("v", t_mask)
            pk.v.memset(vt[:, :], 1.0)

            with ExitStack() as p0:
                sb0 = lambda nm, shape, dt=F32: p0.enter_context(
                    nc.sbuf_tensor(nm, shape, dt))
                ps_tr = p0.enter_context(nc.psum_tensor("ps0_tr", [128, 1024], BF16))
                ps_trb = p0.enter_context(nc.psum_tensor("ps0_trb", [128, 1024], BF16))
                ps_k = p0.enter_context(nc.psum_tensor("ps0_k", [128, 1024], F32))
                ps_kb = p0.enter_context(nc.psum_tensor("ps0_kb", [128, 1024], F32))
                ps_v = p0.enter_context(nc.psum_tensor("ps0_v", [128, 1024], F32))
                wkb = sb0("wkb", [128, 8 * D], BF16)
                wvb = sb0("wvb", [128, 8 * D], BF16)
                x0 = [sb0("x0a", [128, 2 * D], F32), sb0("x0b", [128, 2 * D], F32)]
                lnhold["sink"] = sb0("lnsink0", [128, D], F32)
                xnb = [sb0("xnb0", [128, 2 * D], BF16),
                       sb0("xnb1", [128, 2 * D], BF16)]
                xnf = [sb0("xnf0", [128, 8 * 256], BF16),
                       sb0("xnf1", [128, 8 * 256], BF16)]
                t_x = [None, None]
                t_x[0] = dma_x_256(x0[0], x_seq[0:256, :])
                t_wospill = None
                t_wosp = [None, None]
                t_wld = [None, None]
                t_wrd = [None, None]
                srcs = [(wo, None, dc) for dc in range(8)]
                srcs += [(wk, wkb, dc) for dc in range(8)]
                srcs += [(wv, wvb, dc) for dc in range(8)]
                for i, (wsrc, wdst, dc) in enumerate(srcs):
                    h = i % 2
                    lq = "y" if wdst is None else "g"
                    if t_wrd[h] is not None:
                        pk.pend(lq, t_wrd[h])
                    t_wld[h] = pk.dma(lq, out=x0[1][:, h * D:(h + 1) * D],
                                      in_=wsrc[dc * 128:(dc + 1) * 128, :])
                    pk.pend("v", t_wld[h])
                    if wdst is None:
                        if t_wosp[h] is not None:
                            pk.pend("v", t_wosp[h])
                        pk.v.tensor_copy(xnb[0][:, h * D:(h + 1) * D],
                                         x0[1][:, h * D:(h + 1) * D])
                        t_wrd[h] = pk.mark("v")
                        pk.pend("g", t_wrd[h])
                        t_wosp[h] = pk.dma("g", out=wo_scr[dc, :, :],
                                           in_=xnb[0][:, h * D:(h + 1) * D])
                        t_wospill = t_wosp[h]
                    else:
                        pk.v.tensor_copy(wdst[:, dc * D:(dc + 1) * D],
                                         x0[1][:, h * D:(h + 1) * D])
                        t_wrd[h] = pk.mark("v")
                t_wcast = pk.mark("v")
                pk.pend("v", t_wospill)
                last_x_use = [None, t_wcast]
                trp0 = [ps_tr, ps_trb]
                t_ev0 = [None, None]
                psk = [ps_k, ps_kb]
                t_evk = [None, None]
                t_evac_v = None
                t_fm = [None, None]
                for t in range(17):
                    b = t % 2
                    pb = (t - 1) % 2
                    if t < 16:
                        if t + 1 < 16:
                            t_x[(t + 1) % 2] = dma_x_256(
                                x0[(t + 1) % 2],
                                x_seq[(t + 1) * 256:(t + 2) * 256, :],
                                pend_tok=last_x_use[(t + 1) % 2])
                        pk.pend("v", t_x[b])
                        pk.pend("s", t_x[b])
                        emit_ln(x0[b], [[xnb[b][:, 0:D]], [xnb[b][:, D:2 * D]]])
                        last_x_use[b] = pk.mark("v")
                    t_xn = pk.mark("v")
                    if t >= 1:
                        tp_ = t - 1
                        for grp in range(2):
                            kq = psk[grp]
                            for hpi in range(4):
                                hp = grp * 4 + hpi
                                for dc in range(8):
                                    if dc == 0:
                                        pk.pend("t", t_fm[pb])
                                        if hpi == 0:
                                            if t_evk[grp] is not None:
                                                pk.pend("t", t_evk[grp])
                                            if tp_ == 0 and grp == 0:
                                                pk.pend("t", t_wcast)
                                    pk.t.matmul(
                                        kq[:, hpi * 256:(hpi + 1) * 256],
                                        wkb[:, dc * D + hp * 128:
                                            dc * D + (hp + 1) * 128],
                                        xnf[pb][:, dc * 256:(dc + 1) * 256],
                                        start=(dc == 0), stop=(dc == 7),
                                        nosync=True)
                            t_kg = pk.mark("t")
                            pk.pend("s", t_kg)
                            for hpi in range(4):
                                hp = grp * 4 + hpi
                                pk.s.activation(
                                    kf[:, hp * S + tp_ * 256:
                                       hp * S + (tp_ + 1) * 256],
                                    kq[:, hpi * 256:(hpi + 1) * 256],
                                    AF.Identity, bias=cst["zero"][:, :],
                                    scale=cst["one"][:, :], nosync=(hpi > 0))
                            t_evk[grp] = pk.mark("s")
                        for sub in range(2):
                            for dc in range(8):
                                if dc == 0:
                                    pk.pend("t", t_fm[pb])
                                    if t_evac_v is not None:
                                        pk.pend("t", t_evac_v)
                                for half in range(2):
                                    pk.t.matmul(
                                        ps_v[:, half * 512:(half + 1) * 512],
                                        xnf[pb][:, dc * 256 + sub * 128:
                                                dc * 256 + (sub + 1) * 128],
                                        wvb[:, dc * D + half * 512:
                                            dc * D + (half + 1) * 512],
                                        start=(dc == 0), stop=(dc == 7),
                                        nosync=True)
                            c = tp_ * 2 + sub
                            pk.pend("s", pk.mark("t"))
                            pk.s.activation(
                                vt[:, c * 1040: c * 1040 + 1040]
                                .rearrange("p (h d) -> p h d", d=65)[:, :, 0:64],
                                ps_v[:, 0:1024].rearrange("p (h d) -> p h d", d=64),
                                AF.Identity, bias=cst["zero"][:, :],
                                scale=cst["one"][:, :])
                            t_evac_v = pk.mark("s")
                    if t < 16:
                        for dc in range(8):
                            bk = dc % 2
                            trq = trp0[bk]
                            pk.pend("t", t_xn)
                            if t_ev0[bk] is not None:
                                pk.pend("t", t_ev0[bk])
                            for sub in range(2):
                                pk.t.transpose(
                                    trq[:, sub * 128:(sub + 1) * 128],
                                    xnb[b][:, sub * D + dc * 128:
                                            sub * D + (dc + 1) * 128],
                                    ident[:, :], nosync=True)
                            pk.pend("s", pk.mark("t"))
                            pk.s.activation(xnf[b][:, dc * 256:(dc + 1) * 256],
                                            trq[:, 0:256], AF.Identity,
                                            bias=cst["zero"][:, :],
                                            scale=cst["one"][:, :])
                            t_ev0[bk] = pk.mark("s")
                        t_fm[b] = pk.mark("s")
            t_kv_done = pk.mark("v")
            t_kv_done_s = pk.mark("s")
            pk.barrier()

            with ExitStack() as p0b:
                sb0 = lambda nm, shape, dt=F32: p0b.enter_context(
                    nc.sbuf_tensor(nm, shape, dt))
                ps_tr = p0b.enter_context(nc.psum_tensor("ps0b_tr", [128, 1024], BF16))
                ps_trb = p0b.enter_context(nc.psum_tensor("ps0b_trb", [128, 1024], BF16))
                ps_q = p0b.enter_context(nc.psum_tensor("ps0b_q", [128, 1024], F32))
                ps_qb = p0b.enter_context(nc.psum_tensor("ps0b_qb", [128, 1024], F32))
                wqb = sb0("wqb", [128, 8 * D], BF16)
                x0 = [sb0("x0c", [128, 2 * D], F32), sb0("x0d", [128, 2 * D], F32)]
                lnhold["sink"] = sb0("lnsink1", [128, D], F32)
                xn32 = [sb0("xn32a", [128, 2 * D], F32),
                        sb0("xn32b", [128, 2 * D], F32)]
                xnb = sb0("xnbb", [128, 2 * D], BF16)
                xnf = sb0("xnfb", [128, 8 * 256], BF16)
                qtile = sb0("qtile", [128, 8 * 256], BF16)
                t_wrd = [None, None]
                for dc in range(8):
                    h = dc % 2
                    if t_wrd[h] is not None:
                        pk.pend("y", t_wrd[h])
                    tok = pk.dma("y", out=xn32[0][:, h * D:(h + 1) * D],
                                 in_=wq[dc * 128:(dc + 1) * 128, :])
                    pk.pend("v", tok)
                    pk.v.tensor_copy(wqb[:, dc * D:(dc + 1) * D],
                                     xn32[0][:, h * D:(h + 1) * D])
                    t_wrd[h] = pk.mark("v")
                t_wq = pk.mark("v")
                trpb = [ps_tr, ps_trb]
                t_evb = [None, None]
                psq = [ps_q, ps_qb]
                t_evq = [None, None]
                t_qspill = None
                t_xqspill = [None, None]
                t_x0 = [None, None]
                t_x0[0] = dma_x_256(x0[0], x_own[0:256, :])
                last_x0_use = [None, t_wq]
                t_lnv = [None, None]
                for th in range(9 if phases >= 1 else 0):
                    b = th % 2
                    if th < 8:
                        if th + 1 < 8:
                            t_x0[(th + 1) % 2] = dma_x_256(
                                x0[(th + 1) % 2],
                                x_own[(th + 1) * 256:(th + 2) * 256, :],
                                pend_tok=last_x0_use[(th + 1) % 2])
                        pk.pend("v", t_x0[b])
                        if t_xqspill[b] is not None:
                            pk.pend("v", t_xqspill[b])
                        pk.pend("s", t_x0[b])
                        emit_ln(x0[b], [[xn32[b][:, 0:D], xnb[:, 0:D]],
                                        [xn32[b][:, D:2 * D], xnb[:, D:2 * D]]])
                        last_x0_use[b] = pk.mark("v")
                        t_lnv[b] = last_x0_use[b]
                    t_xn = pk.mark("v")
                    if th >= 1:
                        ph = th - 1
                        pbk = ph % 2
                        ptt, phalf = ph // 2, ph % 2
                        for grp in range(2):
                            qq = psq[grp]
                            for hpi in range(4):
                                hp = grp * 4 + hpi
                                for dc in range(8):
                                    if dc == 0:
                                        pk.pend("t", t_fm)
                                        if hpi == 0:
                                            if t_evq[grp] is not None:
                                                pk.pend("t", t_evq[grp])
                                            if ph == 0 and grp == 0:
                                                pk.pend("t", t_wq)
                                    pk.t.matmul(
                                        qq[:, hpi * 256:(hpi + 1) * 256],
                                        wqb[:, dc * D + hp * 128:
                                            dc * D + (hp + 1) * 128],
                                        xnf[:, dc * 256:(dc + 1) * 256],
                                        start=(dc == 0), stop=(dc == 7),
                                        nosync=True)
                            t_qg = pk.mark("t")
                            pk.pend("s", t_qg)
                            if t_qspill is not None:
                                pk.pend("s", t_qspill)
                            for hpi in range(4):
                                hp = grp * 4 + hpi
                                pk.s.activation(qtile[:, hp * 256:(hp + 1) * 256],
                                                qq[:, hpi * 256:(hpi + 1) * 256],
                                                AF.Identity, bias=cst["zero"][:, :],
                                                scale=cst["one"][:, :],
                                                nosync=(hpi > 0))
                            t_evq[grp] = pk.mark("s")
                        pk.pend("g", pk.mark("s"))
                        for hp in range(8):
                            t_qspill = pk.dma(
                                "g",
                                out=q_scr[ptt, :, hp * 512 + phalf * 256:
                                          hp * 512 + phalf * 256 + 256],
                                in_=qtile[:, hp * 256:(hp + 1) * 256],
                                flush=(hp == 7))
                        pk.pend("g", t_lnv[pbk])
                        t_xqspill[pbk] = pk.dma(
                            "g", out=xn_scr[ptt, :,
                                            phalf * 2 * D:(phalf * 2 + 2) * D],
                            in_=xn32[pbk][:, :])
                    if th < 8:
                        for dc in range(8):
                            bk = dc % 2
                            trq = trpb[bk]
                            pk.pend("t", t_xn)
                            if t_evb[bk] is not None:
                                pk.pend("t", t_evb[bk])
                            for sub in range(2):
                                pk.t.transpose(
                                    trq[:, sub * 128:(sub + 1) * 128],
                                    xnb[:, sub * D + dc * 128:
                                        sub * D + (dc + 1) * 128],
                                    ident[:, :], nosync=True)
                            pk.pend("s", pk.mark("t"))
                            pk.s.activation(xnf[:, dc * 256:(dc + 1) * 256],
                                            trq[:, 0:256], AF.Identity,
                                            bias=cst["zero"][:, :],
                                            scale=cst["one"][:, :])
                            t_evb[bk] = pk.mark("s")
                        t_fm = pk.mark("s")
                t_p0b_done = pk.dmark("dma_g")
                pk.barrier()

            with ExitStack() as p1:
                sb1 = lambda nm, shape, dt=F32: p1.enter_context(
                    nc.sbuf_tensor(nm, shape, dt))
                ps_s = [p1.enter_context(nc.psum_tensor("ps_s0", [128, 1536], F32)),
                        p1.enter_context(nc.psum_tensor("ps_s1", [128, 1536], F32))]
                ps_av = [p1.enter_context(nc.psum_tensor("ps_av0", [128, 512], F32)),
                         p1.enter_context(nc.psum_tensor("ps_av1", [128, 512], F32))]
                woB = sb1("woB", [128, 8 * D], BF16)
                qt0 = sb1("qt0", [128, 8 * 512], BF16)
                qt = [qt0, qt0]
                pbuf = [sb1("p0", [128, 1536], BF16), sb1("p1", [128, 1536], BF16)]
                pm_one = sb1("pm0", [128, 1536], BF16)
                pmbuf = [pm_one, pm_one]
                aot_n = sb1("aot_n", [128, 8 * 512], BF16)
                rbB = sb1("rbB", [64, 512], BF16)
                srl = sb1("srl", [1, 512], F32)
                rb = sb1("rb", [1, 512], BF16)
                ay = sb1("ay", [128, 8 * 512], BF16)
                w32p = [sb1(f"w32p{i}", [128, 256], F32) for i in range(2)]
                wq16p = [sb1(f"wq16p{i}", [128, 256], BF16) for i in range(2)]
                qb1 = [make_quant_bufs(sb1, 256, f"p1{i}") for i in range(2)]
                pieces = []
                for ci in range(22):
                    for hf in range(4):
                        pieces.append((wg[ci * 128:(ci + 1) * 128,
                                          hf * 256:(hf + 1) * 256],
                                       wgq_scr[ci, :, hf * 256:(hf + 1) * 256]))
                        pieces.append((wu[ci * 128:(ci + 1) * 128,
                                          hf * 256:(hf + 1) * 256],
                                       wuq_scr[ci, :, hf * 256:(hf + 1) * 256]))
                for ci in range(8):
                    for sl in range(11):
                        w0 = sl * 256
                        pieces.append((wd[ci * 128:(ci + 1) * 128, w0:w0 + 256],
                                       wdq_scr[ci, :, w0:w0 + 256]))
                for tq in range(4):
                    for sl in range(16):
                        w0 = sl * 256
                        pieces.append((xn_scr[tq, :, w0:w0 + 256],
                                       xq_scr[tq, :, w0:w0 + 256]))
                assert len(pieces) == 176 + 88 + 64
                piece_i = [0]
                piece_tok = [None, None]
                piece_ld = [None, None]
                piece_st = [None, None]

                def emit_piece():
                    s = piece_i[0]
                    if s > len(pieces) + 2:
                        return
                    piece_i[0] = s + 1
                    if s < len(pieces):
                        k = s % 2
                        src, dst = pieces[s]
                        w = src.shape[1]
                        if piece_st[k] is not None:
                            pk.pend("y", piece_st[k]["x2"])
                        piece_ld[k] = pk.dma("y", out=w32p[k][:, 0:w], in_=src)
                    ti = s - 3
                    if 0 <= ti < len(pieces):
                        k = ti % 2
                        src, dst = pieces[ti]
                        w = src.shape[1]
                        if piece_tok[k] is not None:
                            pk.pend("v", piece_tok[k])
                        emit_quant_tail(wq16p[k][:, 0:w], w, qb1[k],
                                        piece_st[k], pool=True)
                        pk.pend("g", pk.mark("v"))
                        piece_tok[k] = pk.dma("g", out=dst,
                                              in_=wq16p[k][:, 0:w])
                    hi = s - 1
                    if 0 <= hi < len(pieces):
                        k = hi % 2
                        src, dst = pieces[hi]
                        w = src.shape[1]
                        pk.pend("v", piece_ld[k])
                        piece_st[k] = emit_quant_head(w32p[k][:, 0:w], w,
                                                      qb1[k], pool=True,
                                                      npool=5)
                t_wo = None
                for dc in range(8):
                    pk.pend("y", t_wospill)
                    t_wo = pk.dma("y", out=woB[:, dc * D:(dc + 1) * D],
                                  in_=wo_scr[dc, :, :], flush=(dc == 7))
                t_q = [None, None]
                pk.pend("y", t_qspill)
                t_q[0] = pk.dma("y", out=qt[0][:, :], in_=q_scr[0, :, :])
                t_q[1] = t_q[0]
                t_ayspill = None
                pk.wait_now("t", t_kv_done)
                pk.wait_now("t", t_kv_done_s)
                prev_tile_t = None
                pk.v.memset(ps_s[0][:, :], 0.0)
                pk.v.memset(ps_s[1][:, :], 0.0, nosync=True)
                t_psz = pk.mark("v")

                for tt in range(4 if phases >= 2 else 0):
                    b = tt % 2
                    nch = {c: (POS[c][tt] + 1) * 4 for c in (0, 1)}
                    ngr = {c: -(-nch[c] // 3) for c in (0, 1)}

                    def _expx(nchv):
                        d0v = nchv - 4
                        glv = [list(range(c0, min(c0 + 3, nchv)))
                               for c0 in range(0, nchv, 3)]
                        ex = 0
                        for gchv in glv:
                            mv = [j for j, c in enumerate(gchv) if c >= d0v]
                            if mv and mv[0] > 0:
                                ex += 1
                        return ex
                    exx = {c: _expx(nch[c]) for c in (0, 1)}
                    snap_t0 = pk.state["t"]["n"]
                    snap_s0 = pk.state["s"]["n"]
                    psnap = (piece_i[0], list(piece_tok), list(piece_ld),
                             list(piece_st))
                    for cls in pk.switch(["t", "s", "v", "g", "y"], 2):
                        piece_i[0] = psnap[0]
                        piece_tok[:] = psnap[1]
                        piece_ld[:] = psnap[2]
                        piece_st[:] = psnap[3]
                        NCH = nch[cls]
                        gcap = min(ngr.values())
                        diag0 = NCH - 4
                        glist = [list(range(c0, min(c0 + 3, NCH)))
                                 for c0 in range(0, NCH, 3)]
                        t_free = {0: None, 1: None}
                        prev_av = {0: None, 1: None}
                        t_avread = {0: [], 1: []}
                        pad_i = 0
                        pending_ht = [None]

                        last_norm = [None]

                        def emit_head_tail():
                            dh, dhp, drow, t_avd = pending_ht[0]
                            pending_ht[0] = None
                            if dh == 0 and prev_tile_t is not None:
                                pk.pend("v", prev_tile_t)
                            pk.pend("s", t_avd)
                            pk.s.activation(srl[0:1, :], ps_av[dh % 2][64:65, :],
                                            AF.Ln)
                            pk.s.activation(rb[0:1, :], srl[0:1, :], AF.Exp,
                                            scale=cst["m1"][0:1, :])
                            bpar = len(glist) % 2
                            if t_free[bpar] is not None:
                                pk.pend("t", t_free[bpar])
                            pk.pend("t", pk.mark("s"))
                            pk.t.matmul(ps_s[bpar][0:64, 0:512], ones1[:, :],
                                        rb[0:1, :], start=True, stop=True,
                                        nosync=True)
                            pk.pend("s", pk.mark("t"))
                            if last_norm[0] is not None:
                                pk.pend("s", last_norm[0])
                            pk.s.activation(rbB[:, :], ps_s[bpar][0:64, 0:512],
                                            AF.Identity,
                                            bias=cst["zero"][0:64, :],
                                            scale=cst["one"][0:64, :])
                            t_free[bpar] = pk.mark("s")
                            pk.pend("v", t_free[bpar])
                            pk.pend("v", t_avd)
                            pk.v.tensor_tensor(
                                aot_n[drow:drow + 64, dhp * 512:(dhp + 1) * 512],
                                ps_av[dh % 2][0:64, :], rbB[:, :], op=ALU.mult)
                            last_norm[0] = pk.mark("v")
                            t_avread[dh % 2] = [last_norm[0], pk.mark("s")]

                        dg = min(3, len(glist) - 1)
                        for h in range(NH):
                            hp, half = h // 2, h % 2
                            row = 64 * half
                            for g, gch in enumerate(glist):
                                par = g % 2
                                if g == dg and pending_ht[0] is not None:
                                    emit_head_tail()
                                if t_free[par] is not None:
                                    pk.pend("t", t_free[par])
                                if h == 0 and g == 0:
                                    pk.pend("t", t_q[b])
                                    if tt == 0:
                                        pk.pend("t", t_psz)
                                for j, c in enumerate(gch):
                                    off = max(0, (c - diag0) * 128)
                                    pk.t.matmul(
                                        ps_s[par][:, j * 512 + off:(j + 1) * 512],
                                        kf[row:row + 64,
                                           hp * S + c * 128: hp * S + (c + 1) * 128],
                                        qt[b][row:row + 64,
                                              hp * 512 + off:(hp + 1) * 512],
                                        start=True, stop=True, nosync=True)
                                pk.pend("s", pk.mark("t"))
                                if prev_av[par] is not None:
                                    pk.pend("s", prev_av[par])
                                glen = len(gch) * 512
                                masked = [j for j, c in enumerate(gch)
                                          if c >= diag0]
                                j0 = masked[0] if masked else 0
                                if masked and j0 > 0:
                                    pk.s.activation(pbuf[par][:, j0 * 512:glen],
                                                    ps_s[par][:, j0 * 512:glen],
                                                    AF.Exp, bias=cst["mM0"][:, :],
                                                    scale=cst["smsc"][:, :])
                                    t_expA = pk.mark("s")
                                    pk.s.activation(pbuf[par][:, 0:j0 * 512],
                                                    ps_s[par][:, 0:j0 * 512],
                                                    AF.Exp, bias=cst["mM0"][:, :],
                                                    scale=cst["smsc"][:, :])
                                    t_exp = pk.mark("s")
                                else:
                                    pk.s.activation(pbuf[par][:, 0:glen],
                                                    ps_s[par][:, 0:glen],
                                                    AF.Exp, bias=cst["mM0"][:, :],
                                                    scale=cst["smsc"][:, :])
                                    t_exp = pk.mark("s")
                                    t_expA = t_exp
                                t_free[par] = t_exp
                                if masked:
                                    pk.pend("v", t_expA)
                                    if prev_av[1 - par] is not None:
                                        pk.pend("v", prev_av[1 - par])
                                    for j in masked:
                                        mi = gch[j] - diag0
                                        off = mi * 128
                                        pk.v.tensor_tensor(
                                            pmbuf[par][:, j * 512 + off:
                                                       j * 512 + off + 128],
                                            pbuf[par][:, j * 512 + off:
                                                      j * 512 + off + 128],
                                            masks_t[:, 0:128], op=ALU.mult)
                                    pk.pend("t", pk.mark("v"))
                                    pk.pend("t", t_exp)
                                else:
                                    pk.pend("t", t_exp)
                                if g == 0:
                                    for tok_ar in t_avread[h % 2]:
                                        pk.pend("t", tok_ar)
                                    t_avread[h % 2] = []
                                for j, c in enumerate(gch):
                                    if c >= diag0:
                                        off = (c - diag0) * 128
                                        pk.t.matmul(
                                            ps_av[h % 2][0:65, off:off + 128],
                                            vt[:, c * 1040 + h * 65:
                                               c * 1040 + h * 65 + 65],
                                            pmbuf[par][:, j * 512 + off:
                                                       j * 512 + off + 128],
                                            start=(c == 0), stop=(c == NCH - 1),
                                            nosync=True)
                                        if off + 128 < 512:
                                            pk.t.matmul(
                                                ps_av[h % 2][0:65, off + 128:512],
                                                vt[:, c * 1040 + h * 65:
                                                   c * 1040 + h * 65 + 65],
                                                pbuf[par][:, j * 512 + off + 128:
                                                          (j + 1) * 512],
                                                start=False, stop=(c == NCH - 1),
                                                nosync=True)
                                    else:
                                        pk.t.matmul(
                                            ps_av[h % 2][0:65, :],
                                            vt[:, c * 1040 + h * 65:
                                               c * 1040 + h * 65 + 65],
                                            pbuf[par][:, j * 512:(j + 1) * 512],
                                            start=(c == 0), stop=(c == NCH - 1),
                                            nosync=True)
                                prev_av[par] = pk.mark("t")
                                if g < gcap:
                                    emit_piece()
                            pending_ht[0] = (h, hp, row, pk.mark("t"))
                        emit_head_tail()
                        t_nall = pk.mark("v")
                        t_ay_evac = {0: None, 1: None}
                        for dc in range(8):
                            if dc % 2 == 0:
                                emit_piece()
                            for ic in range(8):
                                if ic == 0:
                                    pk.pend("t", t_nall)
                                    if t_ay_evac[dc % 2] is not None:
                                        pk.pend("t", t_ay_evac[dc % 2])
                                    if tt == 0:
                                        pk.pend("t", t_wo)
                                pk.t.matmul(
                                    ps_av[dc % 2][:, :],
                                    woB[:, ic * D + dc * 128: ic * D + (dc + 1) * 128],
                                    aot_n[:, ic * 512:(ic + 1) * 512],
                                    start=(ic == 0), stop=(ic == 7), nosync=True)
                            pk.pend("s", pk.mark("t"))
                            if dc == 0 and t_ayspill is not None:
                                pk.pend("s", t_ayspill)
                            pk.s.activation(ay[:, dc * 512:(dc + 1) * 512],
                                            ps_av[dc % 2][:, :], AF.Identity,
                                            bias=bo_t[:, dc:dc + 1],
                                            scale=cst["one"][:, :])
                            t_ay_evac[dc % 2] = pk.mark("s")
                        cur_t = pk.state["t"]["n"] - snap_t0
                        cur_s = pk.state["s"]["n"] - snap_s0
                        fixed_t = cur_t - 2 * NCH * NH
                        fixed_s = cur_s - (len(glist) + exx[cls]) * NH
                        tgt_t = 2 * max(nch.values()) * NH + fixed_t
                        tgt_s = max(ngr[c] + exx[c] for c in (0, 1)) * NH + fixed_s
                        while pk.state["t"]["n"] - snap_t0 < tgt_t:
                            pk.t.matmul(
                                ps_s[0][64:128, pad_i % 512: pad_i % 512 + 1],
                                ones1[:, :], ones1[0:1, 0:1],
                                start=True, stop=True, nosync=True)
                            pad_i += 1
                        padj = 0
                        while pk.state["s"]["n"] - snap_s0 < tgt_s:
                            pk.s.activation(junk[0:1, padj:padj + 1],
                                            cst["one"][0:1, :], AF.Identity,
                                            bias=cst["one"][0:1, :],
                                            scale=cst["one"][0:1, :], nosync=True)
                            padj += 1
                    for k in (0, 1):
                        if piece_st[k] is not None:
                            piece_st[k] = {"x2": pk.mark("s"),
                                           "gcmp": pk.mark("g"),
                                           "npool": piece_st[k]["npool"]}
                        if piece_ld[k] is not None:
                            piece_ld[k] = pk.dmark("dma_y")
                        if piece_tok[k] is not None:
                            piece_tok[k] = pk.dmark("dma_g")
                    prev_tile_t = pk.mark("t")
                    pk.pend("g", pk.mark("s"))
                    t_ayspill = pk.dma("g", out=ay_scr[tt, :, :], in_=ay[:, :])
                    if tt + 1 < 4:
                        pk.pend("y", prev_tile_t)
                        t_q[(tt + 1) % 2] = pk.dma("y", out=qt0[:, :],
                                                   in_=q_scr[tt + 1, :, :])
                t_p1_done = t_ayspill
                pk.barrier()

        with ExitStack() as p2:
            sb2 = lambda nm, shape, dt=F32: p2.enter_context(nc.sbuf_tensor(nm, shape, dt))
            ps_tr2 = p2.enter_context(nc.psum_tensor("ps_tr2", [128, 1024], BF16))
            ps_tr2b = p2.enter_context(nc.psum_tensor("ps_tr2b", [128, 1024], BF16))
            ps_gs = [p2.enter_context(nc.psum_tensor("ps_g0", [128, 512], F32)),
                     p2.enter_context(nc.psum_tensor("ps_g1", [128, 512], F32))]
            ps_us = [p2.enter_context(nc.psum_tensor("ps_u0", [128, 512], F32)),
                     p2.enter_context(nc.psum_tensor("ps_u1", [128, 512], F32))]
            ps_dn = p2.enter_context(nc.psum_tensor("ps_dn", [128, 1024], F32))
            wdqT = sb2("wdqT", [128, 22 * D], BF16)
            xqf = sb2("xqf", [128, 16 * 1024], BF16)
            wtmp16 = sb2("wtmp16", [128, I], BF16)
            slabw_sets = [[sb2("slabg0", [128, 8 * 512], BF16),
                           sb2("slabu0", [128, 8 * 512], BF16)],
                          [sb2("slabg1", [128, 8 * 512], BF16),
                           sb2("slabu1", [128, 8 * 512], BF16)]]
            xq_tm = sb2("xq_tm", [128, 4 * D], BF16)
            gact2 = [sb2("gact0", [128, 512], F32), sb2("gact1", [128, 512], F32)]
            uact2 = [sb2("uact0", [128, 512], F32), sb2("uact1", [128, 512], F32)]
            hbuf2 = [sb2("hbuf0", [128, 512], F32), sb2("hbuf1", [128, 512], F32)]
            hq162 = [sb2("hq160", [128, 512], BF16), sb2("hq161", [128, 512], BF16)]
            hqT = sb2("hqT", [128, 22 * 128], BF16)
            hqT2 = [hqT, sb2("hqTb", [128, 22 * 128], BF16)]
            ayb2 = [sb2("ayb0", [128, 1024], BF16), sb2("ayb1", [128, 1024], BF16)]
            xob2 = [sb2("xob0", [128, 1024], F32), sb2("xob1", [128, 1024], F32)]
            dn32 = sb2("dn32", [128, 1024], F32)
            o32a = sb2("o32a", [128, 1024], F32)
            o32 = sb2("o32", [128, 1024], F32)
            qbufs2 = [make_quant_bufs(sb2, 512, "p2a"), make_quant_bufs(sb2, 512, "p2b")]

            trp = [ps_tr2, ps_tr2b]
            t_ev2 = [None, None]
            tr_i = [0]

            def tr_bank():
                b = tr_i[0] % 2
                tr_i[0] += 1
                return b

            wstg = [slabw_sets[0][0], slabw_sets[0][1]]
            t_stg = [None, None]
            t_q16h = [None, None]
            for hc in range(8 if phases >= 3 else 0):
                sh = hc % 2
                if t_stg[sh] is not None:
                    pk.pend("y", t_stg[sh])
                t_q16h[sh] = pk.dma("y", out=wstg[sh][:, 0:I],
                                    in_=wdq_scr[hc, :, :])
                for icg in range(3):
                    ics = list(range(icg * 8, min(icg * 8 + 8, 22)))
                    b = tr_bank()
                    for j, ic in enumerate(ics):
                        if j == 0:
                            pk.pend("t", t_q16h[sh])
                            if t_ev2[b] is not None:
                                pk.pend("t", t_ev2[b])
                        pk.t.transpose(trp[b][:, j * 128:(j + 1) * 128],
                                       wstg[sh][:, ic * 128:(ic + 1) * 128],
                                       ident[:, :], nosync=True)
                    nic = len(ics)
                    pk.pend("s", pk.mark("t"))
                    pk.s.activation(
                        wdqT[:, :].rearrange("p (ic h) -> p ic h", h=D)
                        [:, icg * 8:icg * 8 + nic, hc * 128:(hc + 1) * 128],
                        trp[b][:, 0:nic * 128].rearrange("p (j h) -> p j h", h=128),
                        AF.Identity, bias=cst["zero"][:, :],
                        scale=cst["one"][:, :])
                    t_ev2[b] = pk.mark("s")
                t_stg[sh] = pk.mark("t")
            t_wdq_prep = pk.mark("t")
            for tt in range(4 if phases >= 3 else 0):
                if tt == 0:
                    pk.pend("y", t_p0b_done)
                else:
                    pk.pend("y", pk.mark("t"))
                t_xq_tm = pk.dma("y", out=xq_tm[:, :], in_=xq_scr[tt, :, :])
                for sub in range(4):
                    gs = tt * 4 + sub
                    b = tr_bank()
                    for dc in range(8):
                        if dc == 0:
                            pk.pend("t", t_xq_tm)
                            if t_ev2[b] is not None:
                                pk.pend("t", t_ev2[b])
                        pk.t.transpose(
                            trp[b][:, dc * 128:(dc + 1) * 128],
                            xq_tm[:, sub * D + dc * 128: sub * D + (dc + 1) * 128],
                            ident[:, :], nosync=True)
                    pk.pend("s", pk.mark("t"))
                    pk.s.activation(xqf[:, gs * 1024:(gs + 1) * 1024], trp[b][:, :],
                                    AF.Identity, bias=cst["zero"][:, :],
                                    scale=cst["one"][:, :])
                    t_ev2[b] = pk.mark("s")
            t_xqf = pk.mark("s")

            SLABS = [(0, 512), (512, 512), (1024, 512), (1536, 512),
                     (2048, 512), (2560, 256)]
            t_gu_sets = [None, None]
            t_hmul2 = [None, None]
            t_hqspill = None
            t_hqsp2 = [None, None]
            hq_queue = []
            t_wt = [t_wdq_prep, t_wdq_prep]
            wti = [0]

            def hq_tail_v(ent):
                (pgs, pi0, pilen, pnchk, ppar, pst) = ent
                hq16 = hq162[ppar]
                if t_hqsp2[ppar] is not None:
                    pk.pend("v", t_hqsp2[ppar])
                emit_quant_tail(hq16[:, 0:pilen], pilen, qbufs2[ppar], pst,
                                pool=True)
                return pk.mark("v")

            def hq_tail_t(ent, t_hq):
                nonlocal t_hqspill
                (pgs, pi0, pilen, pnchk, ppar, pst) = ent
                hq16 = hq162[ppar]
                b = tr_bank()
                for ch in range(pnchk):
                    if ch == 0:
                        pk.pend("t", t_hq)
                        if t_ev2[b] is not None:
                            pk.pend("t", t_ev2[b])
                    pk.t.transpose(trp[b][:, ch * 128:(ch + 1) * 128],
                                   hq16[:, ch * 128:(ch + 1) * 128], ident[:, :],
                                   nosync=True)
                pk.pend("s", pk.mark("t"))
                if t_hqspill is not None:
                    pk.pend("s", t_hqspill)
                pk.s.activation(hqT[:, 0:pnchk * 128], trp[b][:, 0:pnchk * 128],
                                AF.Identity, bias=cst["zero"][:, :],
                                scale=cst["one"][:, :])
                t_ev2[b] = pk.mark("s")
                pk.pend("g", t_ev2[b])
                t_hqspill = pk.dma(
                    "g",
                    out=hqT_scr[pgs, pi0 // 128:(pi0 + pilen) // 128, :, :]
                    .rearrange("c p d -> p c d"),
                    in_=hqT[:, 0:pnchk * 128]
                    .rearrange("p (c d) -> p c d", d=128))
                t_hqsp2[ppar] = t_hqspill

            for si, (i0, ilen) in enumerate(SLABS if phases >= 3 else []):
                nchk = ilen // 128
                slabw = slabw_sets[si % 2]
                for q_i, (wscr, wdst) in enumerate(((wgq_scr, slabw[0]),
                                                    (wuq_scr, slabw[1]))):
                    for ch in range(nchk):
                        wh = wti[0] % 2
                        wti[0] += 1
                        if t_wt[wh] is not None:
                            pk.pend("y", t_wt[wh])
                        t_q16 = pk.dma("y", out=wtmp16[:, wh * D:(wh + 1) * D],
                                       in_=wscr[i0 // 128 + ch, :, :])
                        b = tr_bank()
                        for hc in range(8):
                            if hc == 0:
                                pk.pend("t", t_q16)
                                if t_ev2[b] is not None:
                                    pk.pend("t", t_ev2[b])
                            pk.t.transpose(trp[b][:, hc * 128:(hc + 1) * 128],
                                           wtmp16[:, wh * D + hc * 128:
                                                  wh * D + (hc + 1) * 128],
                                           ident[:, :], nosync=True)
                        t_wt[wh] = pk.mark("t")
                        pk.pend("s", t_wt[wh])
                        if t_gu_sets[si % 2] is not None and ch == 0 and q_i == 0:
                            pk.pend("s", t_gu_sets[si % 2])
                        pk.s.activation(
                            wdst[:, :].rearrange("p (hc i) -> p hc i", i=512)
                            [:, :, ch * 128:(ch + 1) * 128],
                            trp[b][:, 0:1024].rearrange("p (hc i) -> p hc i", i=128),
                            AF.Identity, bias=cst["zero"][:, :],
                            scale=cst["one"][:, :])
                        t_ev2[b] = pk.mark("s")
                t_slab = pk.mark("s")
                t_h_reads = [None, None]
                for gs in range(16):
                    par2 = gs % 2
                    ps_g = ps_gs[par2]
                    ps_u = ps_us[par2]
                    gact = gact2[par2]
                    uact = uact2[par2]
                    hbuf = hbuf2[par2]
                    qbufs = qbufs2[par2]
                    ent = None
                    if len(hq_queue) == 2:
                        ent = hq_queue.pop(0)
                        t_hqv = hq_tail_v(ent)
                    for hc in range(8):
                        if hc == 0:
                            pk.pend("t", t_slab)
                            pk.pend("t", t_xqf)
                            if t_h_reads[par2] is not None:
                                pk.pend("t", t_h_reads[par2])
                        pk.t.matmul(
                            ps_g[:, 0:ilen],
                            xqf[:, gs * 1024 + hc * 128: gs * 1024 + (hc + 1) * 128],
                            slabw[0][:, hc * 512: hc * 512 + ilen],
                            start=(hc == 0), stop=(hc == 7), nosync=True)
                        pk.t.matmul(
                            ps_u[:, 0:ilen],
                            xqf[:, gs * 1024 + hc * 128: gs * 1024 + (hc + 1) * 128],
                            slabw[1][:, hc * 512: hc * 512 + ilen],
                            start=(hc == 0), stop=(hc == 7), nosync=True)
                    t_gu = pk.mark("t")
                    t_gu_sets[si % 2] = t_gu
                    pk.pend("s", t_gu)
                    if t_hmul2[par2] is not None:
                        pk.pend("s", t_hmul2[par2])
                    pk.s.activation(gact[:, 0:ilen], ps_g[:, 0:ilen], AF.Silu)
                    pk.s.activation(uact[:, 0:ilen], ps_u[:, 0:ilen], AF.Identity,
                                    bias=cst["zero"][:, :], scale=cst["one"][:, :])
                    t_h_reads[par2] = pk.mark("s")
                    if ent is not None:
                        hq_tail_t(ent, t_hqv)
                    pk.pend("v", t_h_reads[par2])
                    pk.v.tensor_tensor(hbuf[:, 0:ilen], gact[:, 0:ilen],
                                       uact[:, 0:ilen], op=ALU.mult)
                    t_hmul2[par2] = pk.mark("v")
                    st_h = emit_quant_head(hbuf[:, 0:ilen], ilen, qbufs,
                                           pool=True, npool=5)
                    hq_queue.append((gs, i0, ilen, nchk, par2, st_h))
            while phases >= 3 and hq_queue:
                ent = hq_queue.pop(0)
                hq_tail_t(ent, hq_tail_v(ent))

            t_out_read = None
            t_dnmm = [None, None]
            t_trd = [None, None]
            tok_h2 = [None, None]
            tok_ay2 = [None, None]
            tok_x2 = [None, None]
            t_dn32 = None
            t_add_last = None

            def stage2_pref(n):
                tt2, sub2 = n // 4, n % 4
                p = n % 2
                if n == 0:
                    pk.pend("y", t_hqspill)
                if t_dnmm[p] is not None:
                    pk.pend("y", t_dnmm[p])
                tok_h2[p] = pk.dma("y", out=hqT2[p][:, 0:22 * 128]
                                   .rearrange("p (c d) -> p c d", d=128),
                                   in_=hqT_scr[n, :, :, :]
                                   .rearrange("c p d -> p c d"))
                if n == 0:
                    pk.pend("y", t_p1_done)
                if t_trd[p] is not None:
                    pk.pend("y", t_trd[p])
                tok_ay2[p] = pk.dma("y", out=ayb2[p][:, :]
                                    .rearrange("p (dc t) -> p dc t", t=128),
                                    in_=ay_scr[tt2, :, :]
                                    .rearrange("p (dc t) -> p dc t", t=512)
                                    [:, :, sub2 * 128:(sub2 + 1) * 128])
                if t_add_last is not None:
                    pk.pend("y", t_add_last)
                tok_x2[p] = pk.dma("y", out=xob2[p][:, :],
                                   in_=x_own[tt2 * 512 + sub2 * 128:
                                             tt2 * 512 + (sub2 + 1) * 128, :])

            if phases >= 3:
                stage2_pref(0)
            for n in range(16 if phases >= 3 else 0):
                tt2, sub = n // 4, n % 4
                p = n % 2
                if n + 1 < 16:
                    stage2_pref(n + 1)
                for ic in range(22):
                    if ic == 0:
                        pk.pend("t", tok_h2[p])
                        if t_dn32 is not None:
                            pk.pend("t", t_dn32)
                    for half in range(2):
                        pk.t.matmul(ps_dn[:, half * 512:(half + 1) * 512],
                                    hqT2[p][:, ic * 128:(ic + 1) * 128],
                                    wdqT[:, ic * D + half * 512:
                                         ic * D + (half + 1) * 512],
                                    start=(ic == 0), stop=(ic == 21), nosync=True)
                t_dnmm[p] = pk.mark("t")
                b = tr_bank()
                for dc in range(8):
                    if dc == 0:
                        pk.pend("t", tok_ay2[p])
                        if t_ev2[b] is not None:
                            pk.pend("t", t_ev2[b])
                    pk.t.transpose(trp[b][:, dc * 128:(dc + 1) * 128],
                                   ayb2[p][:, dc * 128:(dc + 1) * 128],
                                   ident[:, :], nosync=True)
                t_trd[p] = pk.mark("t")
                pk.pend("s", t_dnmm[p])
                if t_add_last is not None:
                    pk.pend("s", t_add_last)
                pk.s.activation(dn32[:, :], ps_dn[:, :], AF.Identity,
                                bias=cst["zero"][:, :], scale=cst["one"][:, :])
                t_dn32 = pk.mark("s")
                pk.pend("v", t_dn32)
                pk.pend("v", tok_x2[p])
                if t_out_read is not None:
                    pk.pend("v", t_out_read)
                pk.v.tensor_tensor(o32a[:, :], xob2[p][:, :], dn32[:, :],
                                   op=ALU.add)
                pk.pend("v", t_trd[p])
                pk.v.tensor_tensor(o32[:, :], o32a[:, :], trp[b][:, :],
                                   op=ALU.add)
                t_add_last = pk.mark("v")
                t_ev2[b] = t_add_last
                pk.pend("g", t_add_last)
                t_out_read = pk.dma(
                    "g", out=out[tt2 * 512 + sub * 128:
                                 tt2 * 512 + (sub + 1) * 128, :],
                    in_=o32[:, :], queue_name="dma_out")

        _finish(pk, nc, out)
    return nc


def _finish(pk, nc, out):
    if "dma_out" not in pk.dma_sem_handles:
        z = pk.ctx.enter_context(nc.sbuf_tensor("zfin", [128, D], F32))
        pk.v.memset(z[:, :], 0.0)
        tok = pk.mark("v")
        for r in range(16):
            pk.pend("g", tok)
            pk.dma("g", out=out[r * 128:(r + 1) * 128, :], in_=z[:, :],
                   queue_name="dma_out", flush=False)
    pk.dma_flush_all("g")


def _host_inputs(x, weights):
    (wq_, wk_, wv_, wo_, bo_, wg_, wu_, wd_) = weights
    ident = np.eye(128, dtype=ml_dtypes.bfloat16)
    m = np.zeros((128, 4, 512), dtype=np.float32)
    for mi in range(4):
        o = mi * 128
        for j in range(128):
            m[j, mi, :] = (np.arange(512) >= o + j)
    masks = np.ascontiguousarray(m.reshape(128, 4 * 512)).astype(ml_dtypes.bfloat16)
    common = dict(wq=wq_, wk=wk_, wv=wv_, wo=wo_, bo=bo_, wg=wg_, wu=wu_, wd=wd_,
                  masks=masks, ident_in=ident)
    in_maps = []
    for c in range(8):
        s, cls = c // 2, c % 2
        xs = np.ascontiguousarray(x[s])
        xo = np.ascontiguousarray(
            np.concatenate([xs[p * 512:(p + 1) * 512] for p in POS[cls]], axis=0))
        im = dict(common)
        im["x_seq"] = xs
        im["x_own"] = xo
        in_maps.append(im)
    return in_maps


def kernel(L0_attn_x, L0_attn_Wq, L0_attn_Wk, L0_attn_Wv, L0_attn_Wo, L0_attn_bo,
           L0_ffn_w_gate, L0_ffn_w_up, L0_ffn_w_down):
    x = np.ascontiguousarray(np.asarray(L0_attn_x, dtype=np.float32))
    weights = tuple(np.ascontiguousarray(np.asarray(w, dtype=np.float32)) for w in
                    (L0_attn_Wq, L0_attn_Wk, L0_attn_Wv, L0_attn_Wo, L0_attn_bo,
                     L0_ffn_w_gate, L0_ffn_w_up, L0_ffn_w_down))
    in_maps = _host_inputs(x, weights)
    nc = build_program()
    res = run_bass_kernel_spmd(nc, in_maps, core_ids=list(range(8)))
    outv = np.empty((4, 4096, 1024), dtype=np.float32)
    for c in range(8):
        s, cls = c // 2, c % 2
        o = np.asarray(res.results[c]["out"])
        for j, p in enumerate(POS[cls]):
            outv[s, p * 512:(p + 1) * 512] = o[j * 512:(j + 1) * 512]
    return outv


# version 30
# speedup vs baseline: 1.0141x; 1.0141x over previous
import numpy as np
import ml_dtypes
from contextlib import ExitStack

import concourse.bass as bass
import concourse.mybir as mybir
from concourse.bass_utils import run_bass_kernel_spmd

ENGS = {"v": "vector", "s": "scalar", "t": "tensor", "g": "gpsimd", "y": "sync"}


class _EngProxy:
    def __init__(self, pk, eng_key):
        self._pk = pk
        self._key = eng_key
        self._eng = getattr(pk.nc, ENGS[eng_key])

    def __getattr__(self, name):
        fn = getattr(self._eng, name)
        if not callable(fn):
            return fn
        pk, key = self._pk, self._key

        def wrap(*args, nosync=False, **kwargs):
            st = pk.state[key]
            waits = list(st["pend"])
            st["pend"] = []
            if not nosync and st["n"] > 0:
                waits.append((pk.self_sems[key], st["n"]))
            dedup = {}
            for sem, val in waits:
                kk = id(sem)
                if kk not in dedup or dedup[kk][1] < val:
                    dedup[kk] = (sem, val)
            waits = list(dedup.values())
            for sem, val in waits[1:]:
                self._eng.wait_ge(sem, val)
            ins = fn(*args, **kwargs)
            if not isinstance(ins, bass.BassInstruction):
                return ins
            if waits:
                ins._wait_ge(waits[0][0], waits[0][1])
            ins.then_inc(pk.self_sems[key], 1)
            st["n"] += 1
            return ins

        return wrap


class PK:
    def __init__(self, nc, ctx):
        self.nc = nc
        self.ctx = ctx
        self.self_sems = {}
        self.state = {}
        for k in ("v", "s", "t", "g", "y"):
            self.state[k] = {"n": 0, "pend": []}
        for k in ("v", "s", "t", "g"):
            self.self_sems[k] = ctx.enter_context(nc.semaphore(f"self_{k}"))
        self.dma_sem_handles = {}
        self.dma_counts = {}
        self.v = _EngProxy(self, "v")
        self.s = _EngProxy(self, "s")
        self.t = _EngProxy(self, "t")
        self.g = _EngProxy(self, "g")

    def mark(self, eng_key):
        if eng_key in self.self_sems:
            return (self.self_sems[eng_key], self.state[eng_key]["n"])
        raise KeyError(eng_key)

    def dmark(self, queue_name):
        return (self.dma_sem_handles[queue_name], self.dma_counts[queue_name])

    def pend(self, eng_key, token):
        self.state[eng_key]["pend"].append(token)

    def wait_now(self, eng_key, token):
        eng = getattr(self.nc, ENGS[eng_key])
        eng.wait_ge(token[0], token[1])

    def switch(self, eng_keys, n, index_fn=None, pad=100000):
        nc = self.nc
        engs = [getattr(nc, ENGS[k]) for k in eng_keys]
        regs = []
        for k, eng in zip(eng_keys, engs):
            pid = nc.partition_id(engines=[eng.engine])
            expr = index_fn(pid) if index_fn else (pid % n)
            regs.append(eng.compute_val(expr))
        snap0 = {k: self.state[k]["n"] for k in eng_keys}
        dsnap0 = dict(self.dma_counts)
        finals = []
        dfinals = []
        for i in nc.Switch(engines=engs, index=regs, n=n):
            for k in eng_keys:
                self.state[k]["n"] = snap0[k]
                self.state[k]["pend"] = []
            self.dma_counts.update(dsnap0)
            yield i
            finals.append({k: self.state[k]["n"] for k in eng_keys})
            dfinals.append(dict(self.dma_counts))
        for f in finals[1:]:
            assert f == finals[0], (
                f"switch bodies must issue equal per-engine instruction "
                f"counts; got {finals}")
        for df in dfinals[1:]:
            assert df == dfinals[0], (
                f"switch bodies must issue equal DMA counts; got {dfinals}")
        for k in eng_keys:
            self.state[k]["n"] = finals[0][k]
            self.state[k]["pend"] = []
        self.dma_counts.update(dfinals[0])

    def dma(self, queue_key, out, in_, queue_name=None, flush=True, **kw):
        if queue_name is None:
            queue_name = f"dma_{queue_key}"
        if queue_name not in self.dma_sem_handles:
            self.dma_sem_handles[queue_name] = self.ctx.enter_context(
                self.nc.semaphore(queue_name))
            self.dma_counts[queue_name] = 0
        sem = self.dma_sem_handles[queue_name]
        eng = getattr(self.nc, ENGS[queue_key])
        st = self.state[queue_key]
        pends = st["pend"]
        st["pend"] = []
        for s2, val in pends[1:]:
            eng.wait_ge(s2, val)
        ins = eng.dma_start(out=out, in_=in_, **kw)
        if pends:
            ins._wait_ge(pends[0][0], pends[0][1])
        ins.then_inc(sem, 16)
        self.dma_counts[queue_name] += 16
        tok = (sem, self.dma_counts[queue_name])
        if flush:
            eng.wait_ge(sem, tok[1])
        return tok

    def dma_flush_all(self, queue_key):
        eng = getattr(self.nc, ENGS[queue_key])
        for queue_name, sem in self.dma_sem_handles.items():
            eng.wait_ge(sem, self.dma_counts[queue_name])

    def barrier(self):
        self.dma_flush_all("y")
        self.dma_flush_all("g")
        toks = [(self.self_sems[k], self.state[k]["n"]) for k in ("t", "s", "v")]
        toks += [(self.dma_sem_handles[q], self.dma_counts[q])
                 for q in self.dma_sem_handles]
        for k in ("t", "s", "v", "g", "y"):
            eng = getattr(self.nc, ENGS[k])
            own = self.self_sems.get(k)
            for sem, val in toks:
                if sem is own or val == 0:
                    continue
                eng.wait_ge(sem, val)


F32 = mybir.dt.float32
BF16 = mybir.dt.bfloat16
AF = mybir.ActivationFunctionType
ALU = mybir.AluOpType
AX = mybir.AxisListType.X

S = 4096
D = 1024
NH = 16
I = 2816
M0 = 8.0
SM_SCALE = 0.125
EPS = 1e-5
POS = {0: [0, 1, 6, 7], 1: [2, 3, 4, 5]}
THRESH = [0.25, 0.75, 1.25, 1.75, 2.5, 3.5, 5.0]
CONSTS = (("one", 1.0), ("zero", 0.0), ("mhalf", -0.5), ("m1", -1.0),
          ("smsc", SM_SCALE), ("mM0", -M0))


def build_program(phases=4):
    nc = bass.Bass("TRN2", target_bir_lowering=False)
    ap = lambda t: t.ap()
    x_seq = ap(nc.dram_tensor("x_seq", [S, D], F32, kind="ExternalInput"))
    x_own = ap(nc.dram_tensor("x_own", [2048, D], F32, kind="ExternalInput"))
    wq = ap(nc.dram_tensor("wq", [D, D], F32, kind="ExternalInput"))
    wk = ap(nc.dram_tensor("wk", [D, D], F32, kind="ExternalInput"))
    wv = ap(nc.dram_tensor("wv", [D, D], F32, kind="ExternalInput"))
    wo = ap(nc.dram_tensor("wo", [D, D], F32, kind="ExternalInput"))
    bo = ap(nc.dram_tensor("bo", [D], F32, kind="ExternalInput"))
    wg = ap(nc.dram_tensor("wg", [I, D], F32, kind="ExternalInput"))
    wu = ap(nc.dram_tensor("wu", [I, D], F32, kind="ExternalInput"))
    wd = ap(nc.dram_tensor("wd", [D, I], F32, kind="ExternalInput"))
    masks = ap(nc.dram_tensor("masks", [128, 4 * 512], BF16, kind="ExternalInput"))
    ident_in = ap(nc.dram_tensor("ident_in", [128, 128], BF16, kind="ExternalInput"))
    out = ap(nc.dram_tensor("out", [2048, D], F32, kind="ExternalOutput"))
    q_scr = ap(nc.dram_tensor("q_scr", [4, 128, 8 * 512], BF16, kind="Internal"))
    xq_scr = ap(nc.dram_tensor("xq_scr", [4, 128, 4 * D], BF16, kind="Internal"))
    xn_scr = ap(nc.dram_tensor("xn_scr", [4, 128, 4 * D], F32, kind="Internal"))
    ay_scr = ap(nc.dram_tensor("ay_scr", [4, 128, 8 * 512], BF16, kind="Internal"))
    wo_scr = ap(nc.dram_tensor("wo_scr", [8, 128, D], BF16, kind="Internal"))
    hqT_scr = ap(nc.dram_tensor("hqT_scr", [16, 22, 128, 128], BF16, kind="Internal"))
    wgq_scr = ap(nc.dram_tensor("wgq_scr", [22, 128, D], BF16, kind="Internal"))
    wuq_scr = ap(nc.dram_tensor("wuq_scr", [22, 128, D], BF16, kind="Internal"))
    wdq_scr = ap(nc.dram_tensor("wdq_scr", [8, 128, I], BF16, kind="Internal"))

    with ExitStack() as ctx:
        pk = PK(nc, ctx)
        sbuf = lambda nm, shape, dt=F32: ctx.enter_context(nc.sbuf_tensor(nm, shape, dt))

        bo_t = sbuf("bo_t", [128, 8], F32)
        ones1 = sbuf("ones1", [1, 64], BF16)
        ident = sbuf("ident", [128, 128], BF16)
        junk = sbuf("junk", [128, 256], F32)
        cst = {nm: sbuf(f"c_{nm}", [128, 1], F32) for nm, _ in CONSTS}

        t_ident = pk.dma("y", out=ident[:, :], in_=ident_in[:, :])
        t_bo = None
        for dc in range(8):
            t_bo = pk.dma("y", out=bo_t[:, dc:dc + 1], in_=bo[dc * 128:(dc + 1) * 128],
                          flush=(dc == 7))
        for nm, val in CONSTS:
            pk.v.memset(cst[nm][:, :], val, nosync=True)
        pk.v.memset(ones1[:, :], 1.0, nosync=True)
        t_const = pk.mark("v")
        pk.wait_now("t", t_ident)
        pk.wait_now("t", t_const)
        pk.wait_now("v", t_bo)
        pk.wait_now("s", t_const)

        lnhold = {}
        lns2 = {nm: sbuf(f"ln_{nm}", [128, 2], F32) for nm in
                ("sum", "sq", "m", "msq", "var", "vare", "lnv", "rstd", "nb1")}

        def emit_ln(x_tile, outs):
            x3 = x_tile.rearrange("p (s d) -> p s d", d=D)
            pk.v.tensor_reduce(lns2["sum"][:, :].rearrange("p (s o) -> p s o", o=1),
                               x3, axis=AX, op=ALU.add)
            pk.s.activation(lnhold["sink"][:, :], x_tile[:, 0:D], AF.Square,
                            accum_out=lns2["sq"][:, 0:1])
            pk.s.activation(lnhold["sink"][:, :], x_tile[:, D:2 * D], AF.Square,
                            accum_out=lns2["sq"][:, 1:2])
            pk.v.tensor_scalar(lns2["m"][:, :], lns2["sum"][:, :], 1.0 / D, None,
                               op0=ALU.mult)
            pk.v.tensor_tensor(lns2["msq"][:, :], lns2["m"][:, :], lns2["m"][:, :],
                               op=ALU.mult)
            pk.pend("v", pk.mark("s"))
            pk.v.scalar_tensor_tensor(lns2["var"][:, :], lns2["sq"][:, :], 1.0 / D,
                                      lns2["msq"][:, :], op0=ALU.mult,
                                      op1=ALU.subtract)
            pk.v.tensor_scalar(lns2["vare"][:, :], lns2["var"][:, :], EPS, None,
                               op0=ALU.add)
            pk.pend("s", pk.mark("v"))
            pk.s.activation(lns2["lnv"][:, :], lns2["vare"][:, :], AF.Ln)
            pk.s.activation(lns2["rstd"][:, :], lns2["lnv"][:, :], AF.Exp,
                            scale=cst["mhalf"][:, :])
            pk.pend("v", pk.mark("s"))
            pk.v.tensor_tensor(lns2["nb1"][:, :], lns2["m"][:, :], lns2["rstd"][:, :],
                               op=ALU.mult)
            pk.v.tensor_scalar(lns2["msq"][:, :], lns2["nb1"][:, :], -1.0, None,
                               op0=ALU.mult)
            for sub in range(2):
                for dst in outs[sub]:
                    pk.v.tensor_scalar(dst, x_tile[:, sub * D:(sub + 1) * D],
                                       lns2["rstd"][:, sub:sub + 1],
                                       lns2["msq"][:, sub:sub + 1],
                                       op0=ALU.mult, op1=ALU.add)

        def make_quant_bufs(alloc, width, tag):
            sm = {nm: alloc(f"q_{nm}_{tag}", [128, width // 16], F32)
                  for nm in ("am", "sc", "rc")}
            sm["scb"] = alloc(f"q_scb_{tag}", [128, width // 16], BF16)
            big = [alloc(f"qb_f{i}_{tag}", [128, width], F32) for i in range(2)]
            big += [alloc(f"qb_{i}_{tag}", [128, width], BF16) for i in range(11)]
            return (sm, big, width)

        NPOOL = 4
        QBIG = 2.0 ** 96
        blk16 = lambda t: t.rearrange("p (nb b) -> p nb b", b=16)

        def q_slices(bufs, width):
            (qsm, big, bw) = bufs
            assert width <= bw
            xn, x2 = (b[:, 0:width] for b in big[0:2])
            l1, l2, mx = (b[:, 0:width] for b in big[2:5])
            cbuf = [b[:, 0:width] for b in big[5:12]]
            sgb = big[12][:, 0:width]
            nb = width // 16
            am, sc, rc = (qsm[nm][:, 0:nb] for nm in ("am", "sc", "rc"))
            scb = qsm["scb"][:, 0:nb]
            return xn, x2, l1, l2, mx, cbuf, sgb, nb, am, sc, rc, scb

        def emit_quant_head(src, width, bufs, pool=False, npool=NPOOL):
            xn, x2, l1, l2, mx, cbuf, sgb, nb, am, sc, rc, scb = \
                q_slices(bufs, width)
            bc = lambda t: t.rearrange("p (nb o) -> p nb o", o=1) \
                .broadcast_to([128, nb, 16])
            pk.v.tensor_reduce(am.rearrange("p (nb o) -> p nb o", o=1),
                               src.rearrange("p (nb b) -> p nb b", b=16),
                               axis=AX, op=ALU.max, apply_absolute_value=True)
            pk.v.tensor_scalar(sc, am, 1.0 / 6.0, 1e-12, op0=ALU.mult, op1=ALU.max)
            pk.v.reciprocal(rc, sc)
            pk.v.tensor_copy(scb, sc)
            pk.pend("v", pk.mark("s"))
            pk.v.tensor_tensor(blk16(xn), src.rearrange("p (nb b) -> p nb b", b=16),
                               bc(rc), op=ALU.mult)
            pk.pend("s", pk.mark("v"))
            pk.s.activation(x2, xn, AF.Square)
            pk.s.activation(sgb, xn, AF.Sign, nosync=True)
            st = {"x2": pk.mark("s"), "gcmp": None, "npool": npool}
            if pool:
                gtmp = [xn, l1, l2, mx]
                t_gclamp = None
                for b0 in range(0, npool, 4):
                    bi = list(range(b0, min(b0 + 4, npool)))
                    for i in bi:
                        th = THRESH[i]
                        pk.pend("g", st["x2"])
                        if t_gclamp is not None:
                            pk.pend("g", t_gclamp)
                        pk.g.tensor_scalar(gtmp[i % 4], x2, QBIG,
                                           -th * th * QBIG,
                                           op0=ALU.mult, op1=ALU.add,
                                           nosync=(i % 4 > 0))
                    t_gsub = pk.mark("g")
                    for i in bi:
                        pk.pend("g", t_gsub)
                        pk.g.tensor_scalar(cbuf[i], gtmp[i % 4], 1.0, 0.0,
                                           op0=ALU.min, op1=ALU.max,
                                           nosync=(i % 4 > 0))
                    t_gclamp = pk.mark("g")
                st["gcmp"] = pk.mark("g")
            return st

        def emit_quant_tail(dst, width, bufs, st, pool=False):
            xn, x2, l1, l2, mx, cbuf, sgb, nb, am, sc, rc, scb = \
                q_slices(bufs, width)
            bc = lambda t: t.rearrange("p (nb o) -> p nb o", o=1) \
                .broadcast_to([128, nb, 16])
            lo = st["npool"] if pool else 0
            for i, th in enumerate(THRESH[lo:7]):
                pk.pend("v", st["x2"])
                pk.v.tensor_scalar(cbuf[lo + i], x2, th * th, None,
                                   op0=ALU.is_gt, nosync=(i > 0))
            if pool:
                pk.pend("v", st["gcmp"])
            pk.v.tensor_tensor(l1, cbuf[0], cbuf[1], op=ALU.add)
            pk.v.tensor_tensor(l2, cbuf[2], cbuf[3], op=ALU.add)
            pk.v.tensor_tensor(mx, cbuf[4], cbuf[5], op=ALU.add)
            pk.v.tensor_tensor(cbuf[0], l1, l2, op=ALU.add)
            qa, u = cbuf[3], cbuf[4]
            pk.v.scalar_tensor_tensor(cbuf[1], cbuf[0], 0.5, mx,
                                      op0=ALU.mult, op1=ALU.add)
            pk.v.scalar_tensor_tensor(qa, cbuf[6], 2.0, cbuf[1],
                                      op0=ALU.mult, op1=ALU.add)
            pk.v.tensor_tensor(blk16(u), blk16(qa), bc(scb), op=ALU.mult)
            pk.v.tensor_tensor(dst, u, sgb, op=ALU.mult)

        def emit_quant(src, dst, width, bufs):
            st = emit_quant_head(src, width, bufs)
            emit_quant_tail(dst, width, bufs, st)

        def dma_x_256(dst, src_rows, pend_tok=None):
            tok = None
            for sub in range(2):
                if pend_tok is not None:
                    pk.pend("y", pend_tok)
                    pend_tok = None
                tok = pk.dma("y", out=dst[:, sub * D:(sub + 1) * D],
                             in_=src_rows[sub * 128:(sub + 1) * 128, :],
                             flush=(sub == 1))
            return tok

        with ExitStack() as attn_scope:
            sba = lambda nm, shape, dt=F32: attn_scope.enter_context(
                nc.sbuf_tensor(nm, shape, dt))
            kf = sba("kf", [128, 8 * S], BF16)
            vt = sba("vt", [128, 32 * 1040], BF16)
            masks_t = sba("masks_t", [128, 128], BF16)
            t_mask = pk.dma("y", out=masks_t[:, :], in_=masks[:, 0:128])
            pk.pend("v", t_mask)
            pk.v.memset(vt[:, :], 1.0)

            with ExitStack() as p0:
                sb0 = lambda nm, shape, dt=F32: p0.enter_context(
                    nc.sbuf_tensor(nm, shape, dt))
                ps_tr = p0.enter_context(nc.psum_tensor("ps0_tr", [128, 1024], BF16))
                ps_trb = p0.enter_context(nc.psum_tensor("ps0_trb", [128, 1024], BF16))
                ps_k = p0.enter_context(nc.psum_tensor("ps0_k", [128, 1024], F32))
                ps_kb = p0.enter_context(nc.psum_tensor("ps0_kb", [128, 1024], F32))
                ps_v = p0.enter_context(nc.psum_tensor("ps0_v", [128, 1024], F32))
                wkb = sb0("wkb", [128, 8 * D], BF16)
                wvb = sb0("wvb", [128, 8 * D], BF16)
                x0 = [sb0("x0a", [128, 2 * D], F32), sb0("x0b", [128, 2 * D], F32)]
                lnhold["sink"] = sb0("lnsink0", [128, D], F32)
                xnb = [sb0("xnb0", [128, 2 * D], BF16),
                       sb0("xnb1", [128, 2 * D], BF16)]
                xnf = [sb0("xnf0", [128, 8 * 256], BF16),
                       sb0("xnf1", [128, 8 * 256], BF16)]
                t_x = [None, None]
                t_x[0] = dma_x_256(x0[0], x_seq[0:256, :])
                t_wospill = None
                t_wosp = [None, None]
                t_wld = [None, None]
                t_wrd = [None, None]
                srcs = [(wo, None, dc) for dc in range(8)]
                srcs += [(wk, wkb, dc) for dc in range(8)]
                srcs += [(wv, wvb, dc) for dc in range(8)]
                for i, (wsrc, wdst, dc) in enumerate(srcs):
                    h = i % 2
                    lq = "y" if wdst is None else "g"
                    if t_wrd[h] is not None:
                        pk.pend(lq, t_wrd[h])
                    t_wld[h] = pk.dma(lq, out=x0[1][:, h * D:(h + 1) * D],
                                      in_=wsrc[dc * 128:(dc + 1) * 128, :])
                    pk.pend("v", t_wld[h])
                    if wdst is None:
                        if t_wosp[h] is not None:
                            pk.pend("v", t_wosp[h])
                        pk.v.tensor_copy(xnb[0][:, h * D:(h + 1) * D],
                                         x0[1][:, h * D:(h + 1) * D])
                        t_wrd[h] = pk.mark("v")
                        pk.pend("g", t_wrd[h])
                        t_wosp[h] = pk.dma("g", out=wo_scr[dc, :, :],
                                           in_=xnb[0][:, h * D:(h + 1) * D])
                        t_wospill = t_wosp[h]
                    else:
                        pk.v.tensor_copy(wdst[:, dc * D:(dc + 1) * D],
                                         x0[1][:, h * D:(h + 1) * D])
                        t_wrd[h] = pk.mark("v")
                t_wcast = pk.mark("v")
                pk.pend("v", t_wospill)
                last_x_use = [None, t_wcast]
                trp0 = [ps_tr, ps_trb]
                t_ev0 = [None, None]
                psk = [ps_k, ps_kb]
                t_evk = [None, None]
                t_evac_v = None
                t_fm = [None, None]
                for t in range(17):
                    b = t % 2
                    pb = (t - 1) % 2
                    if t < 16:
                        if t + 1 < 16:
                            t_x[(t + 1) % 2] = dma_x_256(
                                x0[(t + 1) % 2],
                                x_seq[(t + 1) * 256:(t + 2) * 256, :],
                                pend_tok=last_x_use[(t + 1) % 2])
                        pk.pend("v", t_x[b])
                        pk.pend("s", t_x[b])
                        emit_ln(x0[b], [[xnb[b][:, 0:D]], [xnb[b][:, D:2 * D]]])
                        last_x_use[b] = pk.mark("v")
                    t_xn = pk.mark("v")
                    if t >= 1:
                        tp_ = t - 1
                        for grp in range(2):
                            kq = psk[grp]
                            for hpi in range(4):
                                hp = grp * 4 + hpi
                                for dc in range(8):
                                    if dc == 0:
                                        pk.pend("t", t_fm[pb])
                                        if hpi == 0:
                                            if t_evk[grp] is not None:
                                                pk.pend("t", t_evk[grp])
                                            if tp_ == 0 and grp == 0:
                                                pk.pend("t", t_wcast)
                                    pk.t.matmul(
                                        kq[:, hpi * 256:(hpi + 1) * 256],
                                        wkb[:, dc * D + hp * 128:
                                            dc * D + (hp + 1) * 128],
                                        xnf[pb][:, dc * 256:(dc + 1) * 256],
                                        start=(dc == 0), stop=(dc == 7),
                                        nosync=True)
                            t_kg = pk.mark("t")
                            pk.pend("s", t_kg)
                            for hpi in range(4):
                                hp = grp * 4 + hpi
                                pk.s.activation(
                                    kf[:, hp * S + tp_ * 256:
                                       hp * S + (tp_ + 1) * 256],
                                    kq[:, hpi * 256:(hpi + 1) * 256],
                                    AF.Identity, bias=cst["zero"][:, :],
                                    scale=cst["one"][:, :], nosync=(hpi > 0))
                            t_evk[grp] = pk.mark("s")
                        for sub in range(2):
                            for dc in range(8):
                                if dc == 0:
                                    pk.pend("t", t_fm[pb])
                                    if t_evac_v is not None:
                                        pk.pend("t", t_evac_v)
                                for half in range(2):
                                    pk.t.matmul(
                                        ps_v[:, half * 512:(half + 1) * 512],
                                        xnf[pb][:, dc * 256 + sub * 128:
                                                dc * 256 + (sub + 1) * 128],
                                        wvb[:, dc * D + half * 512:
                                            dc * D + (half + 1) * 512],
                                        start=(dc == 0), stop=(dc == 7),
                                        nosync=True)
                            c = tp_ * 2 + sub
                            pk.pend("s", pk.mark("t"))
                            pk.s.activation(
                                vt[:, c * 1040: c * 1040 + 1040]
                                .rearrange("p (h d) -> p h d", d=65)[:, :, 0:64],
                                ps_v[:, 0:1024].rearrange("p (h d) -> p h d", d=64),
                                AF.Identity, bias=cst["zero"][:, :],
                                scale=cst["one"][:, :])
                            t_evac_v = pk.mark("s")
                    if t < 16:
                        for dc in range(8):
                            bk = dc % 2
                            trq = trp0[bk]
                            pk.pend("t", t_xn)
                            if t_ev0[bk] is not None:
                                pk.pend("t", t_ev0[bk])
                            for sub in range(2):
                                pk.t.transpose(
                                    trq[:, sub * 128:(sub + 1) * 128],
                                    xnb[b][:, sub * D + dc * 128:
                                            sub * D + (dc + 1) * 128],
                                    ident[:, :], nosync=True)
                            pk.pend("s", pk.mark("t"))
                            pk.s.activation(xnf[b][:, dc * 256:(dc + 1) * 256],
                                            trq[:, 0:256], AF.Identity,
                                            bias=cst["zero"][:, :],
                                            scale=cst["one"][:, :])
                            t_ev0[bk] = pk.mark("s")
                        t_fm[b] = pk.mark("s")
            t_kv_done = pk.mark("v")
            t_kv_done_s = pk.mark("s")
            pk.barrier()

            with ExitStack() as p0b:
                sb0 = lambda nm, shape, dt=F32: p0b.enter_context(
                    nc.sbuf_tensor(nm, shape, dt))
                ps_tr = p0b.enter_context(nc.psum_tensor("ps0b_tr", [128, 1024], BF16))
                ps_trb = p0b.enter_context(nc.psum_tensor("ps0b_trb", [128, 1024], BF16))
                ps_q = p0b.enter_context(nc.psum_tensor("ps0b_q", [128, 1024], F32))
                ps_qb = p0b.enter_context(nc.psum_tensor("ps0b_qb", [128, 1024], F32))
                wqb = sb0("wqb", [128, 8 * D], BF16)
                x0 = [sb0("x0c", [128, 2 * D], F32), sb0("x0d", [128, 2 * D], F32)]
                lnhold["sink"] = sb0("lnsink1", [128, D], F32)
                xn32 = [sb0("xn32a", [128, 2 * D], F32),
                        sb0("xn32b", [128, 2 * D], F32)]
                xnb = sb0("xnbb", [128, 2 * D], BF16)
                xnf = sb0("xnfb", [128, 8 * 256], BF16)
                qtile = sb0("qtile", [128, 8 * 256], BF16)
                t_wrd = [None, None]
                for dc in range(8):
                    h = dc % 2
                    if t_wrd[h] is not None:
                        pk.pend("y", t_wrd[h])
                    tok = pk.dma("y", out=xn32[0][:, h * D:(h + 1) * D],
                                 in_=wq[dc * 128:(dc + 1) * 128, :])
                    pk.pend("v", tok)
                    pk.v.tensor_copy(wqb[:, dc * D:(dc + 1) * D],
                                     xn32[0][:, h * D:(h + 1) * D])
                    t_wrd[h] = pk.mark("v")
                t_wq = pk.mark("v")
                trpb = [ps_tr, ps_trb]
                t_evb = [None, None]
                psq = [ps_q, ps_qb]
                t_evq = [None, None]
                t_qspill = None
                t_xqspill = [None, None]
                t_x0 = [None, None]
                t_x0[0] = dma_x_256(x0[0], x_own[0:256, :])
                last_x0_use = [None, t_wq]
                t_lnv = [None, None]
                for th in range(9 if phases >= 1 else 0):
                    b = th % 2
                    if th < 8:
                        if th + 1 < 8:
                            t_x0[(th + 1) % 2] = dma_x_256(
                                x0[(th + 1) % 2],
                                x_own[(th + 1) * 256:(th + 2) * 256, :],
                                pend_tok=last_x0_use[(th + 1) % 2])
                        pk.pend("v", t_x0[b])
                        if t_xqspill[b] is not None:
                            pk.pend("v", t_xqspill[b])
                        pk.pend("s", t_x0[b])
                        emit_ln(x0[b], [[xn32[b][:, 0:D], xnb[:, 0:D]],
                                        [xn32[b][:, D:2 * D], xnb[:, D:2 * D]]])
                        last_x0_use[b] = pk.mark("v")
                        t_lnv[b] = last_x0_use[b]
                    t_xn = pk.mark("v")
                    if th >= 1:
                        ph = th - 1
                        pbk = ph % 2
                        ptt, phalf = ph // 2, ph % 2
                        for grp in range(2):
                            qq = psq[grp]
                            for hpi in range(4):
                                hp = grp * 4 + hpi
                                for dc in range(8):
                                    if dc == 0:
                                        pk.pend("t", t_fm)
                                        if hpi == 0:
                                            if t_evq[grp] is not None:
                                                pk.pend("t", t_evq[grp])
                                            if ph == 0 and grp == 0:
                                                pk.pend("t", t_wq)
                                    pk.t.matmul(
                                        qq[:, hpi * 256:(hpi + 1) * 256],
                                        wqb[:, dc * D + hp * 128:
                                            dc * D + (hp + 1) * 128],
                                        xnf[:, dc * 256:(dc + 1) * 256],
                                        start=(dc == 0), stop=(dc == 7),
                                        nosync=True)
                            t_qg = pk.mark("t")
                            pk.pend("s", t_qg)
                            if t_qspill is not None:
                                pk.pend("s", t_qspill)
                            for hpi in range(4):
                                hp = grp * 4 + hpi
                                pk.s.activation(qtile[:, hp * 256:(hp + 1) * 256],
                                                qq[:, hpi * 256:(hpi + 1) * 256],
                                                AF.Identity, bias=cst["zero"][:, :],
                                                scale=cst["one"][:, :],
                                                nosync=(hpi > 0))
                            t_evq[grp] = pk.mark("s")
                        pk.pend("g", pk.mark("s"))
                        for hp in range(8):
                            t_qspill = pk.dma(
                                "g",
                                out=q_scr[ptt, :, hp * 512 + phalf * 256:
                                          hp * 512 + phalf * 256 + 256],
                                in_=qtile[:, hp * 256:(hp + 1) * 256],
                                flush=(hp == 7))
                        pk.pend("g", t_lnv[pbk])
                        t_xqspill[pbk] = pk.dma(
                            "g", out=xn_scr[ptt, :,
                                            phalf * 2 * D:(phalf * 2 + 2) * D],
                            in_=xn32[pbk][:, :])
                    if th < 8:
                        for dc in range(8):
                            bk = dc % 2
                            trq = trpb[bk]
                            pk.pend("t", t_xn)
                            if t_evb[bk] is not None:
                                pk.pend("t", t_evb[bk])
                            for sub in range(2):
                                pk.t.transpose(
                                    trq[:, sub * 128:(sub + 1) * 128],
                                    xnb[:, sub * D + dc * 128:
                                        sub * D + (dc + 1) * 128],
                                    ident[:, :], nosync=True)
                            pk.pend("s", pk.mark("t"))
                            pk.s.activation(xnf[:, dc * 256:(dc + 1) * 256],
                                            trq[:, 0:256], AF.Identity,
                                            bias=cst["zero"][:, :],
                                            scale=cst["one"][:, :])
                            t_evb[bk] = pk.mark("s")
                        t_fm = pk.mark("s")
                t_p0b_done = pk.dmark("dma_g")
                pk.barrier()

            with ExitStack() as p1:
                sb1 = lambda nm, shape, dt=F32: p1.enter_context(
                    nc.sbuf_tensor(nm, shape, dt))
                ps_s = [p1.enter_context(nc.psum_tensor("ps_s0", [128, 1536], F32)),
                        p1.enter_context(nc.psum_tensor("ps_s1", [128, 1536], F32))]
                ps_av = [p1.enter_context(nc.psum_tensor("ps_av0", [128, 512], F32)),
                         p1.enter_context(nc.psum_tensor("ps_av1", [128, 512], F32))]
                woB = sb1("woB", [128, 8 * D], BF16)
                qt0 = sb1("qt0", [128, 8 * 512], BF16)
                qt = [qt0, qt0]
                pbuf = [sb1("p0", [128, 1536], BF16), sb1("p1", [128, 1536], BF16)]
                pm_one = sb1("pm0", [128, 1536], BF16)
                pmbuf = [pm_one, pm_one]
                aot_n = sb1("aot_n", [128, 8 * 512], BF16)
                rbB = sb1("rbB", [64, 512], BF16)
                srl = sb1("srl", [1, 512], F32)
                rb = sb1("rb", [1, 512], BF16)
                ay = sb1("ay", [128, 8 * 512], BF16)
                w32p = [sb1(f"w32p{i}", [128, 256], F32) for i in range(2)]
                wq16p = [sb1(f"wq16p{i}", [128, 256], BF16) for i in range(2)]
                qb1 = [make_quant_bufs(sb1, 256, f"p1{i}") for i in range(2)]
                pieces = []
                for ci in range(22):
                    for hf in range(4):
                        pieces.append((wg[ci * 128:(ci + 1) * 128,
                                          hf * 256:(hf + 1) * 256],
                                       wgq_scr[ci, :, hf * 256:(hf + 1) * 256]))
                        pieces.append((wu[ci * 128:(ci + 1) * 128,
                                          hf * 256:(hf + 1) * 256],
                                       wuq_scr[ci, :, hf * 256:(hf + 1) * 256]))
                for ci in range(8):
                    for sl in range(11):
                        w0 = sl * 256
                        pieces.append((wd[ci * 128:(ci + 1) * 128, w0:w0 + 256],
                                       wdq_scr[ci, :, w0:w0 + 256]))
                for tq in range(4):
                    for sl in range(16):
                        w0 = sl * 256
                        pieces.append((xn_scr[tq, :, w0:w0 + 256],
                                       xq_scr[tq, :, w0:w0 + 256]))
                assert len(pieces) == 176 + 88 + 64
                piece_i = [0]
                piece_tok = [None, None]
                piece_ld = [None, None]
                piece_st = [None, None]

                def emit_piece():
                    s = piece_i[0]
                    if s > len(pieces) + 2:
                        return
                    piece_i[0] = s + 1
                    if s < len(pieces):
                        k = s % 2
                        src, dst = pieces[s]
                        w = src.shape[1]
                        if piece_st[k] is not None:
                            pk.pend("y", piece_st[k]["x2"])
                        piece_ld[k] = pk.dma("y", out=w32p[k][:, 0:w], in_=src)
                    ti = s - 3
                    if 0 <= ti < len(pieces):
                        k = ti % 2
                        src, dst = pieces[ti]
                        w = src.shape[1]
                        if piece_tok[k] is not None:
                            pk.pend("v", piece_tok[k])
                        emit_quant_tail(wq16p[k][:, 0:w], w, qb1[k],
                                        piece_st[k], pool=True)
                        pk.pend("g", pk.mark("v"))
                        piece_tok[k] = pk.dma("g", out=dst,
                                              in_=wq16p[k][:, 0:w])
                    hi = s - 1
                    if 0 <= hi < len(pieces):
                        k = hi % 2
                        src, dst = pieces[hi]
                        w = src.shape[1]
                        pk.pend("v", piece_ld[k])
                        piece_st[k] = emit_quant_head(w32p[k][:, 0:w], w,
                                                      qb1[k], pool=True,
                                                      npool=5)
                t_wo = None
                for dc in range(8):
                    pk.pend("y", t_wospill)
                    t_wo = pk.dma("y", out=woB[:, dc * D:(dc + 1) * D],
                                  in_=wo_scr[dc, :, :], flush=(dc == 7))
                t_q = [None, None]
                pk.pend("y", t_qspill)
                t_q[0] = pk.dma("y", out=qt[0][:, :], in_=q_scr[0, :, :])
                t_q[1] = t_q[0]
                t_ayspill = None
                pk.wait_now("t", t_kv_done)
                pk.wait_now("t", t_kv_done_s)
                prev_tile_t = None
                pk.v.memset(ps_s[0][:, :], 0.0)
                pk.v.memset(ps_s[1][:, :], 0.0, nosync=True)
                t_psz = pk.mark("v")

                for tt in range(4 if phases >= 2 else 0):
                    b = tt % 2
                    nch = {c: (POS[c][tt] + 1) * 4 for c in (0, 1)}
                    ngr = {c: -(-nch[c] // 3) for c in (0, 1)}

                    def _expx(nchv):
                        d0v = nchv - 4
                        glv = [list(range(c0, min(c0 + 3, nchv)))
                               for c0 in range(0, nchv, 3)]
                        ex = 0
                        for gchv in glv:
                            mv = [j for j, c in enumerate(gchv) if c >= d0v]
                            if mv and mv[0] > 0:
                                ex += 1
                        return ex
                    exx = {c: _expx(nch[c]) for c in (0, 1)}
                    snap_t0 = pk.state["t"]["n"]
                    snap_s0 = pk.state["s"]["n"]
                    psnap = (piece_i[0], list(piece_tok), list(piece_ld),
                             list(piece_st))
                    for cls in pk.switch(["t", "s", "v", "g", "y"], 2):
                        piece_i[0] = psnap[0]
                        piece_tok[:] = psnap[1]
                        piece_ld[:] = psnap[2]
                        piece_st[:] = psnap[3]
                        NCH = nch[cls]
                        gcap = min(ngr.values())
                        diag0 = NCH - 4
                        glist = [list(range(c0, min(c0 + 3, NCH)))
                                 for c0 in range(0, NCH, 3)]
                        t_free = {0: None, 1: None}
                        prev_av = {0: None, 1: None}
                        t_avread = {0: [], 1: []}
                        pad_i = 0
                        pending_ht = [None]

                        last_norm = [None]

                        def emit_head_tail():
                            dh, dhp, drow, t_avd = pending_ht[0]
                            pending_ht[0] = None
                            if dh == 0 and prev_tile_t is not None:
                                pk.pend("v", prev_tile_t)
                            pk.pend("s", t_avd)
                            pk.s.activation(srl[0:1, :], ps_av[dh % 2][64:65, :],
                                            AF.Ln)
                            pk.s.activation(rb[0:1, :], srl[0:1, :], AF.Exp,
                                            scale=cst["m1"][0:1, :])
                            bpar = len(glist) % 2
                            if t_free[bpar] is not None:
                                pk.pend("t", t_free[bpar])
                            pk.pend("t", pk.mark("s"))
                            pk.t.matmul(ps_s[bpar][0:64, 0:512], ones1[:, :],
                                        rb[0:1, :], start=True, stop=True,
                                        nosync=True)
                            pk.pend("s", pk.mark("t"))
                            if last_norm[0] is not None:
                                pk.pend("s", last_norm[0])
                            pk.s.activation(rbB[:, :], ps_s[bpar][0:64, 0:512],
                                            AF.Identity,
                                            bias=cst["zero"][0:64, :],
                                            scale=cst["one"][0:64, :])
                            t_free[bpar] = pk.mark("s")
                            pk.pend("v", t_free[bpar])
                            pk.pend("v", t_avd)
                            pk.v.tensor_tensor(
                                aot_n[drow:drow + 64, dhp * 512:(dhp + 1) * 512],
                                ps_av[dh % 2][0:64, :], rbB[:, :], op=ALU.mult)
                            last_norm[0] = pk.mark("v")
                            t_avread[dh % 2] = [last_norm[0], pk.mark("s")]

                        dg = min(2, len(glist) - 1)
                        for h in range(NH):
                            hp, half = h // 2, h % 2
                            row = 64 * half
                            sc_tok = [None] * len(glist)

                            def emit_scores(g2):
                                gch2 = glist[g2]
                                par2_ = g2 % 2
                                if t_free[par2_] is not None:
                                    pk.pend("t", t_free[par2_])
                                if h == 0 and g2 == 0:
                                    pk.pend("t", t_q[b])
                                    if tt == 0:
                                        pk.pend("t", t_psz)
                                for j2, c2 in enumerate(gch2):
                                    off2 = max(0, (c2 - diag0) * 128)
                                    pk.t.matmul(
                                        ps_s[par2_][:, j2 * 512 + off2:
                                                    (j2 + 1) * 512],
                                        kf[row:row + 64,
                                           hp * S + c2 * 128:
                                           hp * S + (c2 + 1) * 128],
                                        qt[b][row:row + 64,
                                              hp * 512 + off2:(hp + 1) * 512],
                                        start=True, stop=True, nosync=True)
                                sc_tok[g2] = pk.mark("t")

                            emit_scores(0)
                            for g, gch in enumerate(glist):
                                par = g % 2
                                pk.pend("s", sc_tok[g])
                                if prev_av[par] is not None:
                                    pk.pend("s", prev_av[par])
                                glen = len(gch) * 512
                                masked = [j for j, c in enumerate(gch)
                                          if c >= diag0]
                                j0 = masked[0] if masked else 0
                                if masked and j0 > 0:
                                    pk.s.activation(pbuf[par][:, j0 * 512:glen],
                                                    ps_s[par][:, j0 * 512:glen],
                                                    AF.Exp, bias=cst["mM0"][:, :],
                                                    scale=cst["smsc"][:, :])
                                    t_expA = pk.mark("s")
                                    pk.s.activation(pbuf[par][:, 0:j0 * 512],
                                                    ps_s[par][:, 0:j0 * 512],
                                                    AF.Exp, bias=cst["mM0"][:, :],
                                                    scale=cst["smsc"][:, :])
                                    t_exp = pk.mark("s")
                                else:
                                    pk.s.activation(pbuf[par][:, 0:glen],
                                                    ps_s[par][:, 0:glen],
                                                    AF.Exp, bias=cst["mM0"][:, :],
                                                    scale=cst["smsc"][:, :])
                                    t_exp = pk.mark("s")
                                    t_expA = t_exp
                                t_free[par] = t_exp
                                if g == dg and pending_ht[0] is not None:
                                    emit_head_tail()
                                if masked:
                                    pk.pend("v", t_expA)
                                    if prev_av[1 - par] is not None:
                                        pk.pend("v", prev_av[1 - par])
                                    for j in masked:
                                        mi = gch[j] - diag0
                                        off = mi * 128
                                        pk.v.tensor_tensor(
                                            pmbuf[par][:, j * 512 + off:
                                                       j * 512 + off + 128],
                                            pbuf[par][:, j * 512 + off:
                                                      j * 512 + off + 128],
                                            masks_t[:, 0:128], op=ALU.mult)
                                    t_mask_done = pk.mark("v")
                                else:
                                    t_mask_done = None
                                if g + 1 < len(glist):
                                    emit_scores(g + 1)
                                if t_mask_done is not None:
                                    pk.pend("t", t_mask_done)
                                    pk.pend("t", t_exp)
                                else:
                                    pk.pend("t", t_exp)
                                if g == 0:
                                    for tok_ar in t_avread[h % 2]:
                                        pk.pend("t", tok_ar)
                                    t_avread[h % 2] = []
                                for j, c in enumerate(gch):
                                    if c >= diag0:
                                        off = (c - diag0) * 128
                                        pk.t.matmul(
                                            ps_av[h % 2][0:65, off:off + 128],
                                            vt[:, c * 1040 + h * 65:
                                               c * 1040 + h * 65 + 65],
                                            pmbuf[par][:, j * 512 + off:
                                                       j * 512 + off + 128],
                                            start=(c == 0), stop=(c == NCH - 1),
                                            nosync=True)
                                        if off + 128 < 512:
                                            pk.t.matmul(
                                                ps_av[h % 2][0:65, off + 128:512],
                                                vt[:, c * 1040 + h * 65:
                                                   c * 1040 + h * 65 + 65],
                                                pbuf[par][:, j * 512 + off + 128:
                                                          (j + 1) * 512],
                                                start=False, stop=(c == NCH - 1),
                                                nosync=True)
                                    else:
                                        pk.t.matmul(
                                            ps_av[h % 2][0:65, :],
                                            vt[:, c * 1040 + h * 65:
                                               c * 1040 + h * 65 + 65],
                                            pbuf[par][:, j * 512:(j + 1) * 512],
                                            start=(c == 0), stop=(c == NCH - 1),
                                            nosync=True)
                                prev_av[par] = pk.mark("t")
                                if g < gcap:
                                    emit_piece()
                            pending_ht[0] = (h, hp, row, pk.mark("t"))
                        emit_head_tail()
                        t_nall = pk.mark("v")
                        t_ay_evac = {0: None, 1: None}
                        for dc in range(8):
                            if dc % 2 == 0:
                                emit_piece()
                            for ic in range(8):
                                if ic == 0:
                                    pk.pend("t", t_nall)
                                    if t_ay_evac[dc % 2] is not None:
                                        pk.pend("t", t_ay_evac[dc % 2])
                                    if tt == 0:
                                        pk.pend("t", t_wo)
                                pk.t.matmul(
                                    ps_av[dc % 2][:, :],
                                    woB[:, ic * D + dc * 128: ic * D + (dc + 1) * 128],
                                    aot_n[:, ic * 512:(ic + 1) * 512],
                                    start=(ic == 0), stop=(ic == 7), nosync=True)
                            pk.pend("s", pk.mark("t"))
                            if dc == 0 and t_ayspill is not None:
                                pk.pend("s", t_ayspill)
                            pk.s.activation(ay[:, dc * 512:(dc + 1) * 512],
                                            ps_av[dc % 2][:, :], AF.Identity,
                                            bias=bo_t[:, dc:dc + 1],
                                            scale=cst["one"][:, :])
                            t_ay_evac[dc % 2] = pk.mark("s")
                        cur_t = pk.state["t"]["n"] - snap_t0
                        cur_s = pk.state["s"]["n"] - snap_s0
                        fixed_t = cur_t - 2 * NCH * NH
                        fixed_s = cur_s - (len(glist) + exx[cls]) * NH
                        tgt_t = 2 * max(nch.values()) * NH + fixed_t
                        tgt_s = max(ngr[c] + exx[c] for c in (0, 1)) * NH + fixed_s
                        while pk.state["t"]["n"] - snap_t0 < tgt_t:
                            pk.t.matmul(
                                ps_s[0][64:128, pad_i % 512: pad_i % 512 + 1],
                                ones1[:, :], ones1[0:1, 0:1],
                                start=True, stop=True, nosync=True)
                            pad_i += 1
                        padj = 0
                        while pk.state["s"]["n"] - snap_s0 < tgt_s:
                            pk.s.activation(junk[0:1, padj:padj + 1],
                                            cst["one"][0:1, :], AF.Identity,
                                            bias=cst["one"][0:1, :],
                                            scale=cst["one"][0:1, :], nosync=True)
                            padj += 1
                    for k in (0, 1):
                        if piece_st[k] is not None:
                            piece_st[k] = {"x2": pk.mark("s"),
                                           "gcmp": pk.mark("g"),
                                           "npool": piece_st[k]["npool"]}
                        if piece_ld[k] is not None:
                            piece_ld[k] = pk.dmark("dma_y")
                        if piece_tok[k] is not None:
                            piece_tok[k] = pk.dmark("dma_g")
                    prev_tile_t = pk.mark("t")
                    pk.pend("g", pk.mark("s"))
                    t_ayspill = pk.dma("g", out=ay_scr[tt, :, :], in_=ay[:, :])
                    if tt + 1 < 4:
                        pk.pend("y", prev_tile_t)
                        t_q[(tt + 1) % 2] = pk.dma("y", out=qt0[:, :],
                                                   in_=q_scr[tt + 1, :, :])
                t_p1_done = t_ayspill
                pk.barrier()

        with ExitStack() as p2:
            sb2 = lambda nm, shape, dt=F32: p2.enter_context(nc.sbuf_tensor(nm, shape, dt))
            ps_tr2 = p2.enter_context(nc.psum_tensor("ps_tr2", [128, 1024], BF16))
            ps_tr2b = p2.enter_context(nc.psum_tensor("ps_tr2b", [128, 1024], BF16))
            ps_gs = [p2.enter_context(nc.psum_tensor("ps_g0", [128, 512], F32)),
                     p2.enter_context(nc.psum_tensor("ps_g1", [128, 512], F32))]
            ps_us = [p2.enter_context(nc.psum_tensor("ps_u0", [128, 512], F32)),
                     p2.enter_context(nc.psum_tensor("ps_u1", [128, 512], F32))]
            ps_dn = p2.enter_context(nc.psum_tensor("ps_dn", [128, 1024], F32))
            wdqT = sb2("wdqT", [128, 22 * D], BF16)
            xqf = sb2("xqf", [128, 16 * 1024], BF16)
            wtmp16 = sb2("wtmp16", [128, I], BF16)
            slabw_sets = [[sb2("slabg0", [128, 8 * 512], BF16),
                           sb2("slabu0", [128, 8 * 512], BF16)],
                          [sb2("slabg1", [128, 8 * 512], BF16),
                           sb2("slabu1", [128, 8 * 512], BF16)]]
            xq_tm = sb2("xq_tm", [128, 4 * D], BF16)
            gact2 = [sb2("gact0", [128, 512], F32), sb2("gact1", [128, 512], F32)]
            uact2 = [sb2("uact0", [128, 512], F32), sb2("uact1", [128, 512], F32)]
            hbuf2 = [sb2("hbuf0", [128, 512], F32), sb2("hbuf1", [128, 512], F32)]
            hq162 = [sb2("hq160", [128, 512], BF16), sb2("hq161", [128, 512], BF16)]
            hqT = sb2("hqT", [128, 22 * 128], BF16)
            hqT2 = [hqT, sb2("hqTb", [128, 22 * 128], BF16)]
            ayb2 = [sb2("ayb0", [128, 1024], BF16), sb2("ayb1", [128, 1024], BF16)]
            xob2 = [sb2("xob0", [128, 1024], F32), sb2("xob1", [128, 1024], F32)]
            dn32 = sb2("dn32", [128, 1024], F32)
            o32a = sb2("o32a", [128, 1024], F32)
            o32 = sb2("o32", [128, 1024], F32)
            qbufs2 = [make_quant_bufs(sb2, 512, "p2a"), make_quant_bufs(sb2, 512, "p2b")]

            trp = [ps_tr2, ps_tr2b]
            t_ev2 = [None, None]
            tr_i = [0]

            def tr_bank():
                b = tr_i[0] % 2
                tr_i[0] += 1
                return b

            wstg = [slabw_sets[0][0], slabw_sets[0][1]]
            t_stg = [None, None]
            t_q16h = [None, None]
            for hc in range(8 if phases >= 3 else 0):
                sh = hc % 2
                if t_stg[sh] is not None:
                    pk.pend("y", t_stg[sh])
                t_q16h[sh] = pk.dma("y", out=wstg[sh][:, 0:I],
                                    in_=wdq_scr[hc, :, :])
                for icg in range(3):
                    ics = list(range(icg * 8, min(icg * 8 + 8, 22)))
                    b = tr_bank()
                    for j, ic in enumerate(ics):
                        if j == 0:
                            pk.pend("t", t_q16h[sh])
                            if t_ev2[b] is not None:
                                pk.pend("t", t_ev2[b])
                        pk.t.transpose(trp[b][:, j * 128:(j + 1) * 128],
                                       wstg[sh][:, ic * 128:(ic + 1) * 128],
                                       ident[:, :], nosync=True)
                    nic = len(ics)
                    pk.pend("s", pk.mark("t"))
                    pk.s.activation(
                        wdqT[:, :].rearrange("p (ic h) -> p ic h", h=D)
                        [:, icg * 8:icg * 8 + nic, hc * 128:(hc + 1) * 128],
                        trp[b][:, 0:nic * 128].rearrange("p (j h) -> p j h", h=128),
                        AF.Identity, bias=cst["zero"][:, :],
                        scale=cst["one"][:, :])
                    t_ev2[b] = pk.mark("s")
                t_stg[sh] = pk.mark("t")
            t_wdq_prep = pk.mark("t")
            for tt in range(4 if phases >= 3 else 0):
                if tt == 0:
                    pk.pend("y", t_p0b_done)
                else:
                    pk.pend("y", pk.mark("t"))
                t_xq_tm = pk.dma("y", out=xq_tm[:, :], in_=xq_scr[tt, :, :])
                for sub in range(4):
                    gs = tt * 4 + sub
                    b = tr_bank()
                    for dc in range(8):
                        if dc == 0:
                            pk.pend("t", t_xq_tm)
                            if t_ev2[b] is not None:
                                pk.pend("t", t_ev2[b])
                        pk.t.transpose(
                            trp[b][:, dc * 128:(dc + 1) * 128],
                            xq_tm[:, sub * D + dc * 128: sub * D + (dc + 1) * 128],
                            ident[:, :], nosync=True)
                    pk.pend("s", pk.mark("t"))
                    pk.s.activation(xqf[:, gs * 1024:(gs + 1) * 1024], trp[b][:, :],
                                    AF.Identity, bias=cst["zero"][:, :],
                                    scale=cst["one"][:, :])
                    t_ev2[b] = pk.mark("s")
            t_xqf = pk.mark("s")

            SLABS = [(0, 512), (512, 512), (1024, 512), (1536, 512),
                     (2048, 512), (2560, 256)]
            t_gu_sets = [None, None]
            t_hmul2 = [None, None]
            t_hqspill = None
            t_hqsp2 = [None, None]
            hq_queue = []
            t_wt = [t_wdq_prep, t_wdq_prep]
            wti = [0]

            def hq_tail_v(ent):
                (pgs, pi0, pilen, pnchk, ppar, pst) = ent
                hq16 = hq162[ppar]
                if t_hqsp2[ppar] is not None:
                    pk.pend("v", t_hqsp2[ppar])
                emit_quant_tail(hq16[:, 0:pilen], pilen, qbufs2[ppar], pst,
                                pool=True)
                return pk.mark("v")

            def hq_tail_t(ent, t_hq):
                nonlocal t_hqspill
                (pgs, pi0, pilen, pnchk, ppar, pst) = ent
                hq16 = hq162[ppar]
                b = tr_bank()
                for ch in range(pnchk):
                    if ch == 0:
                        pk.pend("t", t_hq)
                        if t_ev2[b] is not None:
                            pk.pend("t", t_ev2[b])
                    pk.t.transpose(trp[b][:, ch * 128:(ch + 1) * 128],
                                   hq16[:, ch * 128:(ch + 1) * 128], ident[:, :],
                                   nosync=True)
                pk.pend("s", pk.mark("t"))
                if t_hqspill is not None:
                    pk.pend("s", t_hqspill)
                pk.s.activation(hqT[:, 0:pnchk * 128], trp[b][:, 0:pnchk * 128],
                                AF.Identity, bias=cst["zero"][:, :],
                                scale=cst["one"][:, :])
                t_ev2[b] = pk.mark("s")
                pk.pend("g", t_ev2[b])
                t_hqspill = pk.dma(
                    "g",
                    out=hqT_scr[pgs, pi0 // 128:(pi0 + pilen) // 128, :, :]
                    .rearrange("c p d -> p c d"),
                    in_=hqT[:, 0:pnchk * 128]
                    .rearrange("p (c d) -> p c d", d=128))
                t_hqsp2[ppar] = t_hqspill

            for si, (i0, ilen) in enumerate(SLABS if phases >= 3 else []):
                nchk = ilen // 128
                slabw = slabw_sets[si % 2]
                for q_i, (wscr, wdst) in enumerate(((wgq_scr, slabw[0]),
                                                    (wuq_scr, slabw[1]))):
                    for ch in range(nchk):
                        wh = wti[0] % 2
                        wti[0] += 1
                        if t_wt[wh] is not None:
                            pk.pend("y", t_wt[wh])
                        t_q16 = pk.dma("y", out=wtmp16[:, wh * D:(wh + 1) * D],
                                       in_=wscr[i0 // 128 + ch, :, :])
                        b = tr_bank()
                        for hc in range(8):
                            if hc == 0:
                                pk.pend("t", t_q16)
                                if t_ev2[b] is not None:
                                    pk.pend("t", t_ev2[b])
                            pk.t.transpose(trp[b][:, hc * 128:(hc + 1) * 128],
                                           wtmp16[:, wh * D + hc * 128:
                                                  wh * D + (hc + 1) * 128],
                                           ident[:, :], nosync=True)
                        t_wt[wh] = pk.mark("t")
                        pk.pend("s", t_wt[wh])
                        if t_gu_sets[si % 2] is not None and ch == 0 and q_i == 0:
                            pk.pend("s", t_gu_sets[si % 2])
                        pk.s.activation(
                            wdst[:, :].rearrange("p (hc i) -> p hc i", i=512)
                            [:, :, ch * 128:(ch + 1) * 128],
                            trp[b][:, 0:1024].rearrange("p (hc i) -> p hc i", i=128),
                            AF.Identity, bias=cst["zero"][:, :],
                            scale=cst["one"][:, :])
                        t_ev2[b] = pk.mark("s")
                t_slab = pk.mark("s")
                t_h_reads = [None, None]
                for gs in range(16):
                    par2 = gs % 2
                    ps_g = ps_gs[par2]
                    ps_u = ps_us[par2]
                    gact = gact2[par2]
                    uact = uact2[par2]
                    hbuf = hbuf2[par2]
                    qbufs = qbufs2[par2]
                    ent = None
                    if len(hq_queue) == 2:
                        ent = hq_queue.pop(0)
                        t_hqv = hq_tail_v(ent)
                    for hc in range(8):
                        if hc == 0:
                            pk.pend("t", t_slab)
                            pk.pend("t", t_xqf)
                            if t_h_reads[par2] is not None:
                                pk.pend("t", t_h_reads[par2])
                        pk.t.matmul(
                            ps_g[:, 0:ilen],
                            xqf[:, gs * 1024 + hc * 128: gs * 1024 + (hc + 1) * 128],
                            slabw[0][:, hc * 512: hc * 512 + ilen],
                            start=(hc == 0), stop=(hc == 7), nosync=True)
                        pk.t.matmul(
                            ps_u[:, 0:ilen],
                            xqf[:, gs * 1024 + hc * 128: gs * 1024 + (hc + 1) * 128],
                            slabw[1][:, hc * 512: hc * 512 + ilen],
                            start=(hc == 0), stop=(hc == 7), nosync=True)
                    t_gu = pk.mark("t")
                    t_gu_sets[si % 2] = t_gu
                    pk.pend("s", t_gu)
                    if t_hmul2[par2] is not None:
                        pk.pend("s", t_hmul2[par2])
                    pk.s.activation(gact[:, 0:ilen], ps_g[:, 0:ilen], AF.Silu)
                    pk.s.activation(uact[:, 0:ilen], ps_u[:, 0:ilen], AF.Identity,
                                    bias=cst["zero"][:, :], scale=cst["one"][:, :])
                    t_h_reads[par2] = pk.mark("s")
                    if ent is not None:
                        hq_tail_t(ent, t_hqv)
                    pk.pend("v", t_h_reads[par2])
                    pk.v.tensor_tensor(hbuf[:, 0:ilen], gact[:, 0:ilen],
                                       uact[:, 0:ilen], op=ALU.mult)
                    t_hmul2[par2] = pk.mark("v")
                    st_h = emit_quant_head(hbuf[:, 0:ilen], ilen, qbufs,
                                           pool=True, npool=5)
                    hq_queue.append((gs, i0, ilen, nchk, par2, st_h))
            while phases >= 3 and hq_queue:
                ent = hq_queue.pop(0)
                hq_tail_t(ent, hq_tail_v(ent))

            t_out_read = None
            t_dnmm = [None, None]
            t_trd = [None, None]
            tok_h2 = [None, None]
            tok_ay2 = [None, None]
            tok_x2 = [None, None]
            t_dn32 = None
            t_add_last = None

            def stage2_pref(n):
                tt2, sub2 = n // 4, n % 4
                p = n % 2
                if n == 0:
                    pk.pend("y", t_hqspill)
                if t_dnmm[p] is not None:
                    pk.pend("y", t_dnmm[p])
                tok_h2[p] = pk.dma("y", out=hqT2[p][:, 0:22 * 128]
                                   .rearrange("p (c d) -> p c d", d=128),
                                   in_=hqT_scr[n, :, :, :]
                                   .rearrange("c p d -> p c d"))
                if n == 0:
                    pk.pend("y", t_p1_done)
                if t_trd[p] is not None:
                    pk.pend("y", t_trd[p])
                tok_ay2[p] = pk.dma("y", out=ayb2[p][:, :]
                                    .rearrange("p (dc t) -> p dc t", t=128),
                                    in_=ay_scr[tt2, :, :]
                                    .rearrange("p (dc t) -> p dc t", t=512)
                                    [:, :, sub2 * 128:(sub2 + 1) * 128])
                if t_add_last is not None:
                    pk.pend("y", t_add_last)
                tok_x2[p] = pk.dma("y", out=xob2[p][:, :],
                                   in_=x_own[tt2 * 512 + sub2 * 128:
                                             tt2 * 512 + (sub2 + 1) * 128, :])

            if phases >= 3:
                stage2_pref(0)
            for n in range(16 if phases >= 3 else 0):
                tt2, sub = n // 4, n % 4
                p = n % 2
                if n + 1 < 16:
                    stage2_pref(n + 1)
                for ic in range(22):
                    if ic == 0:
                        pk.pend("t", tok_h2[p])
                        if t_dn32 is not None:
                            pk.pend("t", t_dn32)
                    for half in range(2):
                        pk.t.matmul(ps_dn[:, half * 512:(half + 1) * 512],
                                    hqT2[p][:, ic * 128:(ic + 1) * 128],
                                    wdqT[:, ic * D + half * 512:
                                         ic * D + (half + 1) * 512],
                                    start=(ic == 0), stop=(ic == 21), nosync=True)
                t_dnmm[p] = pk.mark("t")
                b = tr_bank()
                for dc in range(8):
                    if dc == 0:
                        pk.pend("t", tok_ay2[p])
                        if t_ev2[b] is not None:
                            pk.pend("t", t_ev2[b])
                    pk.t.transpose(trp[b][:, dc * 128:(dc + 1) * 128],
                                   ayb2[p][:, dc * 128:(dc + 1) * 128],
                                   ident[:, :], nosync=True)
                t_trd[p] = pk.mark("t")
                pk.pend("s", t_dnmm[p])
                if t_add_last is not None:
                    pk.pend("s", t_add_last)
                pk.s.activation(dn32[:, :], ps_dn[:, :], AF.Identity,
                                bias=cst["zero"][:, :], scale=cst["one"][:, :])
                t_dn32 = pk.mark("s")
                pk.pend("v", t_dn32)
                pk.pend("v", tok_x2[p])
                if t_out_read is not None:
                    pk.pend("v", t_out_read)
                pk.v.tensor_tensor(o32a[:, :], xob2[p][:, :], dn32[:, :],
                                   op=ALU.add)
                pk.pend("v", t_trd[p])
                pk.v.tensor_tensor(o32[:, :], o32a[:, :], trp[b][:, :],
                                   op=ALU.add)
                t_add_last = pk.mark("v")
                t_ev2[b] = t_add_last
                pk.pend("g", t_add_last)
                t_out_read = pk.dma(
                    "g", out=out[tt2 * 512 + sub * 128:
                                 tt2 * 512 + (sub + 1) * 128, :],
                    in_=o32[:, :], queue_name="dma_out")

        _finish(pk, nc, out)
    return nc


def _finish(pk, nc, out):
    if "dma_out" not in pk.dma_sem_handles:
        z = pk.ctx.enter_context(nc.sbuf_tensor("zfin", [128, D], F32))
        pk.v.memset(z[:, :], 0.0)
        tok = pk.mark("v")
        for r in range(16):
            pk.pend("g", tok)
            pk.dma("g", out=out[r * 128:(r + 1) * 128, :], in_=z[:, :],
                   queue_name="dma_out", flush=False)
    pk.dma_flush_all("g")


def _host_inputs(x, weights):
    (wq_, wk_, wv_, wo_, bo_, wg_, wu_, wd_) = weights
    ident = np.eye(128, dtype=ml_dtypes.bfloat16)
    m = np.zeros((128, 4, 512), dtype=np.float32)
    for mi in range(4):
        o = mi * 128
        for j in range(128):
            m[j, mi, :] = (np.arange(512) >= o + j)
    masks = np.ascontiguousarray(m.reshape(128, 4 * 512)).astype(ml_dtypes.bfloat16)
    common = dict(wq=wq_, wk=wk_, wv=wv_, wo=wo_, bo=bo_, wg=wg_, wu=wu_, wd=wd_,
                  masks=masks, ident_in=ident)
    in_maps = []
    for c in range(8):
        s, cls = c // 2, c % 2
        xs = np.ascontiguousarray(x[s])
        xo = np.ascontiguousarray(
            np.concatenate([xs[p * 512:(p + 1) * 512] for p in POS[cls]], axis=0))
        im = dict(common)
        im["x_seq"] = xs
        im["x_own"] = xo
        in_maps.append(im)
    return in_maps


def kernel(L0_attn_x, L0_attn_Wq, L0_attn_Wk, L0_attn_Wv, L0_attn_Wo, L0_attn_bo,
           L0_ffn_w_gate, L0_ffn_w_up, L0_ffn_w_down):
    x = np.ascontiguousarray(np.asarray(L0_attn_x, dtype=np.float32))
    weights = tuple(np.ascontiguousarray(np.asarray(w, dtype=np.float32)) for w in
                    (L0_attn_Wq, L0_attn_Wk, L0_attn_Wv, L0_attn_Wo, L0_attn_bo,
                     L0_ffn_w_gate, L0_ffn_w_up, L0_ffn_w_down))
    in_maps = _host_inputs(x, weights)
    nc = build_program()
    res = run_bass_kernel_spmd(nc, in_maps, core_ids=list(range(8)))
    outv = np.empty((4, 4096, 1024), dtype=np.float32)
    for c in range(8):
        s, cls = c // 2, c % 2
        o = np.asarray(res.results[c]["out"])
        for j, p in enumerate(POS[cls]):
            outv[s, p * 512:(p + 1) * 512] = o[j * 512:(j + 1) * 512]
    return outv
